# version 13
# baseline (speedup 1.0000x reference)
import dataclasses
import numpy as np
import ml_dtypes

import concourse.bass as bass
import concourse.mybir as mybir
from concourse.alu_op_type import AluOpType as OP
from concourse.bass_utils import run_bass_kernel_spmd

F32 = mybir.dt.float32
BF16 = mybir.dt.bfloat16
I32 = mybir.dt.int32
AF = mybir.ActivationFunctionType
AX = mybir.AxisListType.X

D = 2048
HID = 5632
NH = 16
S_LEN = 2048
B_CORE = 2
TT = 1024
NTILES = 4
NSUB = 8
NTB = 2
KD = 16
NHB = 44
NFS = 8
EPS = 1e-5
RD = 1.0 / D


class Ch:
    def __init__(self, nc, name):
        self.sem = nc.alloc_semaphore(name)
        self.name = name
        self.n = 0

    def inc(self, instr, k=1):
        instr.then_inc(self.sem, k)
        self.n += k
        return (self, self.n)

    def ev(self):
        return (self, self.n)


class Bld:
    def __init__(self):
        nc = bass.Bass("TRN2", target_bir_lowering=False, debug=False)
        self.nc = nc
        self.T, self.V, self.S, self.G, self.Y = nc.tensor, nc.vector, nc.scalar, nc.gpsimd, nc.sync
        self.cT, self.cV, self.cS, self.cG = Ch(nc, "cT"), Ch(nc, "cV"), Ch(nc, "cS"), Ch(nc, "cG")
        self.dchs = {}
        self._wc = {}

    def dch(self, name):
        if name not in self.dchs:
            self.dchs[name] = Ch(self.nc, "q_" + name)
        return self.dchs[name]

    def quiesce(self, eng=None):
        eng = eng or self.Y
        for ch in self.dchs.values():
            self.wait(eng, ch.ev())

    def wait(self, eng, *evs):
        key = id(eng)
        flat = []
        for ev in evs:
            if isinstance(ev, list):
                flat.extend(ev)
            else:
                flat.append(ev)
        for ev in flat:
            if ev is None:
                continue
            ch, n = ev
            if n <= 0:
                continue
            if self._wc.get((key, ch.name), 0) >= n:
                continue
            eng.wait_ge(ch.sem, n)
            self._wc[(key, ch.name)] = n

    def barrier(self):
        self.nc.all_engine_barrier()
        for eng in (self.T, self.V, self.S, self.G, self.Y):
            for ch in [self.cT, self.cV, self.cS, self.cG] + list(self.dchs.values()):
                k = (id(eng), ch.name)
                self._wc[k] = max(self._wc.get(k, 0), ch.n)


def bcol(ap, n):
    return dataclasses.replace(ap, ap=[ap.ap[0], [0, n]])


def build(taps=(), n_a_tiles=NTILES, n_b_tiles=NTILES, dbg_strict=0):
    b = Bld()
    nc = b.nc
    T, V, S, G, Y = b.T, b.V, b.S, b.G, b.Y
    cT, cV, cS, cG = b.cT, b.cV, b.cS, b.cG

    x_d = nc.dram_tensor("L0_attn_x", [B_CORE, S_LEN, D], F32, kind="ExternalInput").ap()
    wq_d = nc.dram_tensor("L0_attn_Wq", [D, D], F32, kind="ExternalInput").ap()
    wv_d = nc.dram_tensor("L0_attn_Wv", [D, D], F32, kind="ExternalInput").ap()
    wg_d = nc.dram_tensor("L0_attn_Wg", [D, D], F32, kind="ExternalInput").ap()
    wo_d = nc.dram_tensor("L0_attn_Wo", [D, D], F32, kind="ExternalInput").ap()
    beta_d = nc.dram_tensor("L0_attn_beta", [NH], F32, kind="ExternalInput").ap()
    lng_d = nc.dram_tensor("L0_attn_ln_g", [D], F32, kind="ExternalInput").ap()
    lnb_d = nc.dram_tensor("L0_attn_ln_b", [D], F32, kind="ExternalInput").ap()
    qg_d = nc.dram_tensor("L0_ffn_wq_gate", [HID, D], I32, kind="ExternalInput").ap()
    sg_d = nc.dram_tensor("L0_ffn_s_gate", [HID], F32, kind="ExternalInput").ap()
    qu_d = nc.dram_tensor("L0_ffn_wq_up", [HID, D], I32, kind="ExternalInput").ap()
    su_d = nc.dram_tensor("L0_ffn_s_up", [HID], F32, kind="ExternalInput").ap()
    qd_d = nc.dram_tensor("L0_ffn_wq_down", [D, HID], I32, kind="ExternalInput").ap()
    sd_d = nc.dram_tensor("L0_ffn_s_down", [D], F32, kind="ExternalInput").ap()
    out_d = nc.dram_tensor("out", [B_CORE, S_LEN, D], F32, kind="ExternalOutput").ap()

    xf = x_d.rearrange("b s d -> (b s) d")
    outf = out_d.rearrange("b s d -> (b s) d")
    x2_d = nc.dram_tensor("x2_scr", [B_CORE * S_LEN, D], F32, kind="Internal").ap()
    wgc_d = nc.dram_tensor("wg_cache", [NHB * 128, KD * 128], BF16, kind="Internal").ap()
    wac_d = nc.dram_tensor("wa_cache", [3 * KD * 128, KD * 128], BF16, kind="Internal").ap()
    woc_d = nc.dram_tensor("wo_cache", [NFS * 128, KD * 256], BF16, kind="Internal").ap()
    wuc_d = nc.dram_tensor("wu_cache", [NHB * 128, KD * 128], BF16, kind="Internal").ap()
    wdc_d = nc.dram_tensor("wd_cache", [4 * 11 * 128, 4 * 512], BF16, kind="Internal").ap()

    eye_d = nc.inline_tensor(np.eye(128).astype(ml_dtypes.bfloat16), "eye_c").ap()
    onesc_d = nc.inline_tensor(np.ones((128, 1), dtype=ml_dtypes.bfloat16), "onesc_c").ap()
    onesr_d = nc.inline_tensor(np.ones((1, 128), dtype=np.float32), "onesr_c").ap()

    taps = set(taps)
    tap_t = {}
    for name, shape, dt in [
        ("dbg_att", [128, NSUB * D], BF16), ("dbg_smal", [128, 128], F32), ("dbg_fin", [128, 64], F32),
        ("dbg_xstg", [128, 4 * D], F32), ("dbg_xnstg", [128, 2 * D], BF16),
        ("xnT", [128, KD * TT], BF16), ("statesT", [128, NH * TT], BF16),
        ("ygatedT", [128, NH * TT], BF16), ("x2", [TT, D], F32),
        ("xn2T", [128, KD * TT], BF16), ("hT", [128, NHB * TT], BF16),
    ]:
        if name in taps:
            tap_t[name] = nc.dram_tensor("tap_" + name, shape, dt, kind="ExternalOutput").ap()

    eye_sb = nc.alloc_sbuf_tensor("eye_sb", [128, 128], BF16)
    onesc_sb = nc.alloc_sbuf_tensor("onesc_sb", [128, 1], BF16)
    onesr_sb = nc.alloc_sbuf_tensor("onesr_sb", [1, 128], F32)
    beta_sb = nc.alloc_sbuf_tensor("beta_sb", [1, NH], F32)
    lamr_sb = nc.alloc_sbuf_tensor("lamr_sb", [1, NH], F32)
    lam_bc = nc.alloc_sbuf_tensor("lam_bc", [128, NH], F32)
    lng_col = nc.alloc_sbuf_tensor("lng_col", [128, KD], F32)
    lnb_col = nc.alloc_sbuf_tensor("lnb_col", [128, KD], F32)
    sg_col = nc.alloc_sbuf_tensor("sg_col", [128, NHB], F32)
    su_col = nc.alloc_sbuf_tensor("su_col", [128, NHB], F32)
    sd_col = nc.alloc_sbuf_tensor("sd_col", [128, KD], F32)
    nsg_col = nc.alloc_sbuf_tensor("nsg_col", [128, NHB], F32)
    nsu_col = nc.alloc_sbuf_tensor("nsu_col", [128, NHB], F32)
    nsd_col = nc.alloc_sbuf_tensor("nsd_col", [128, KD], F32)
    carry_sb = nc.alloc_sbuf_tensor("carry_sb", [128, NH], F32)
    eps_col = nc.alloc_sbuf_tensor("eps_col", [128, 1], F32)
    neg_col = nc.alloc_sbuf_tensor("neg_col", [128, 1], F32)
    one_col = nc.alloc_sbuf_tensor("one_col", [128, 1], F32)

    PB = [nc.alloc_psum_tensor(f"pb{i}", [128, 512], F32) for i in range(8)]
    bank_free = [None] * 8

    def col_load(dst, src1d):
        return G.dma_start(dst, src1d.rearrange("(p one) -> p one", one=1))

    def ln_group_s(stat, c0, w, ev_stats):
        c1 = c0 + w
        b.wait(S, *ev_stats)
        cS.inc(S.mul(stat[:, 16 + c0:16 + c1], stat[:, c0:c1], RD))
        ev1 = cS.inc(S.mul(stat[:, 24 + c0:24 + c1], stat[:, 8 + c0:8 + c1], RD))
        b.wait(S, ev1)
        cS.inc(S.activation(stat[:, 32 + c0:32 + c1], stat[:, 16 + c0:16 + c1], AF.Square))
        ev2 = cS.inc(S.mul(stat[:, 32 + c0:32 + c1], stat[:, 32 + c0:32 + c1], -1.0))
        b.wait(S, ev2)
        ev3 = cS.inc(S.add(stat[:, 32 + c0:32 + c1], stat[:, 32 + c0:32 + c1], eps_col[:]))
        b.wait(S, ev3)
        for k in range(w):
            ev4 = cS.inc(S.activation(stat[:, 40 + c0 + k:41 + c0 + k],
                                      stat[:, 24 + c0 + k:25 + c0 + k], AF.Ln,
                                      bias=stat[:, 32 + c0 + k:33 + c0 + k]))
        b.wait(S, ev4)
        ev5 = cS.inc(S.mul(stat[:, 40 + c0:40 + c1], stat[:, 40 + c0:40 + c1], -0.5))
        b.wait(S, ev5)
        ev6 = cS.inc(S.activation(stat[:, 48 + c0:48 + c1], stat[:, 40 + c0:40 + c1], AF.Exp))
        b.wait(S, ev6)
        for k in range(w):
            ev7 = cS.inc(S.mul(stat[:, 56 + c0 + k:57 + c0 + k],
                               stat[:, 16 + c0 + k:17 + c0 + k],
                               stat[:, 48 + c0 + k:49 + c0 + k]))
        b.wait(S, ev7)
        ev_nb = cS.inc(S.mul(stat[:, 56 + c0:56 + c1], stat[:, 56 + c0:56 + c1], -1.0))
        return ev_nb

    def ln_group(stat, c0, w, ev_stats, eng=None, ch=None):
        eng = eng or V
        ch = ch or cV
        c1 = c0 + w
        b.wait(eng, *ev_stats)
        ev_mu = ch.inc(eng.tensor_scalar(stat[:, 16 + c0:16 + c1], stat[:, c0:c1], RD, None, OP.mult))
        ev_msq = ch.inc(eng.tensor_scalar(stat[:, 24 + c0:24 + c1], stat[:, 8 + c0:8 + c1],
                                          RD, None, OP.mult))
        b.wait(eng, ev_msq)
        ev_m2 = ch.inc(eng.tensor_mul(stat[:, 32 + c0:32 + c1], stat[:, 16 + c0:16 + c1],
                                      stat[:, 16 + c0:16 + c1]))
        b.wait(eng, ev_m2)
        ev_nv = ch.inc(eng.tensor_sub(stat[:, 32 + c0:32 + c1], stat[:, 32 + c0:32 + c1],
                                      stat[:, 24 + c0:24 + c1]))
        b.wait(S, ev_nv)
        ev_lg = cS.inc(S.activation(stat[:, 40 + c0:40 + c1], stat[:, 32 + c0:32 + c1],
                                    AF.Ln, bias=eps_col[:], scale=neg_col[:]))
        b.wait(eng, ev_lg)
        ev_h = ch.inc(eng.tensor_scalar(stat[:, 32 + c0:32 + c1], stat[:, 40 + c0:40 + c1],
                                        -0.5, None, OP.mult))
        b.wait(S, ev_h)
        ev_rc = cS.inc(S.activation(stat[:, 48 + c0:48 + c1], stat[:, 32 + c0:32 + c1], AF.Exp))
        b.wait(eng, ev_rc)
        ev_t = ch.inc(eng.tensor_mul(stat[:, 56 + c0:56 + c1], stat[:, 16 + c0:16 + c1],
                                     stat[:, 48 + c0:48 + c1]))
        b.wait(eng, ev_t)
        ev_nb = ch.inc(eng.tensor_scalar(stat[:, 56 + c0:56 + c1], stat[:, 56 + c0:56 + c1],
                                         -1.0, None, OP.mult))
        return ev_nb

    with nc.Block() as _blk:
        b.dch("cc").inc(G.dma_start(eye_sb[:], eye_d[:]), 16)
        b.dch("cc").inc(G.dma_start(onesc_sb[:], onesc_d[:]), 16)
        b.dch("cc").inc(G.dma_start(onesr_sb[:], onesr_d[:]), 16)
        b.dch("cc").inc(G.dma_start(beta_sb[:], beta_d.rearrange("(one h) -> one h", one=1)), 16)
        for k in range(KD):
            b.dch("cc").inc(col_load(lng_col[:, k:k + 1], lng_d[k * 128:(k + 1) * 128]), 16)
            b.dch("cc").inc(col_load(lnb_col[:, k:k + 1], lnb_d[k * 128:(k + 1) * 128]), 16)
            b.dch("cc").inc(col_load(sd_col[:, k:k + 1], sd_d[k * 128:(k + 1) * 128]), 16)
        for k in range(NHB):
            b.dch("cc").inc(col_load(sg_col[:, k:k + 1], sg_d[k * 128:(k + 1) * 128]), 16)
            b.dch("cc").inc(col_load(su_col[:, k:k + 1], su_d[k * 128:(k + 1) * 128]), 16)
        ev_cst = b.dch("cc").ev()
        b.wait(S, ev_cst)
        ev_lam = cS.inc(S.activation(lamr_sb[:], beta_sb[:], AF.Sigmoid))
        b.wait(V, ev_cst)
        cV.inc(V.tensor_scalar(nsg_col[:], sg_col[:], -128.0, None, OP.mult))
        cV.inc(V.tensor_scalar(nsu_col[:], su_col[:], -128.0, None, OP.mult))
        cV.inc(V.tensor_scalar(nsd_col[:], sd_col[:], -128.0, None, OP.mult))
        cV.inc(V.memset(eps_col[:], EPS))
        cV.inc(V.memset(neg_col[:], -1.0))
        cV.inc(V.memset(one_col[:], 1.0))
        b.wait(T, ev_lam)
        ev_t = cT.inc(T.matmul(PB[0][:, 0:NH], onesr_sb[:], lamr_sb[:], start=True, stop=True))
        b.wait(V, ev_t)
        cV.inc(V.tensor_copy(lam_bc[:], PB[0][:, 0:NH]))
        b.quiesce(G)
        b.barrier()

        with (
            nc.sbuf_tensor("p0_int", [128, 3 * D], I32) as p0_int,
            nc.sbuf_tensor("p0_intd", [128, 3 * 2816], I32) as p0_intd,
            nc.sbuf_tensor("p0_deq", [128, 3 * D], BF16) as p0_deq,
            nc.sbuf_tensor("p0_deqd", [128, 3 * 2816], BF16) as p0_deqd,
            nc.sbuf_tensor("p0_out", [128, 3 * D], BF16) as p0_out,
            nc.sbuf_tensor("p0_outd", [128, 3 * 2816], BF16) as p0_outd,
        ):
            sfree = {}
            grot = 0
            _sc0 = nc.enter_named_scope("p0gu", False)
            for wi, (src, scol, ncol, dst) in enumerate(
                    [(qg_d, sg_col, nsg_col, wgc_d), (qu_d, su_col, nsu_col, wuc_d)]):
                for hb in range(NHB):
                    i = wi * NHB + hb
                    sl = i % 3
                    b.wait(G, sfree.get(("int", sl)))
                    ev_ld = b.dch(f"p0i{sl}").inc(G.dma_start(p0_int[:, sl * D:(sl + 1) * D],
                                               src[hb * 128:(hb + 1) * 128, :]), 16)
                    b.wait(S, ev_ld, sfree.get(("deq", sl)))
                    ev_deq = cS.inc(S.activation(p0_deq[:, sl * D:(sl + 1) * D],
                                                 p0_int[:, sl * D:(sl + 1) * D], AF.Identity,
                                                 bias=ncol[:, hb:hb + 1], scale=scol[:, hb:hb + 1]))
                    sfree[("int", sl)] = ev_deq
                    ev_e = None
                    for g in range(4):
                        rb = grot % 6
                        grot += 1
                        b.wait(T, ev_deq, bank_free[rb])
                        for kk in range(4):
                            kd = g * 4 + kk
                            ins = T.matmul(PB[rb][:, kk * 128:(kk + 1) * 128],
                                           p0_deq[:, sl * D + kd * 128: sl * D + (kd + 1) * 128],
                                           eye_sb[:], start=True, stop=True)
                        ev_tr = cT.inc(ins)
                        eng, ch = (V, cV) if g % 2 == 0 else (S, cS)
                        b.wait(eng, ev_tr, sfree.get(("out", sl)))
                        if eng is V:
                            ev_e = ch.inc(V.tensor_copy(
                                p0_out[:, sl * D + g * 512: sl * D + (g + 1) * 512], PB[rb][:, 0:512]))
                        else:
                            ev_e = ch.inc(S.activation(
                                p0_out[:, sl * D + g * 512: sl * D + (g + 1) * 512], PB[rb][:, 0:512],
                                AF.Copy))
                        bank_free[rb] = ev_e
                        if g == 3:
                            ev_e4 = [bank_free[(grot - 4 + k) % 6] for k in range(4)]
                    sfree[("deq", sl)] = ev_e
                    b.wait(Y, *ev_e4)
                    sfree[("out", sl)] = b.dch(f"p0o{sl}").inc(
                        Y.dma_start(dst[hb * 128:(hb + 1) * 128, :], p0_out[:, sl * D:(sl + 1) * D]), 16)
            nc.leave_named_scope("p0gu", _sc0[0], False)
            _sc1 = nc.enter_named_scope("p0d", False)
            for ec in range(KD):
                es, ee = ec // 4, ec % 4
                for half in range(2):
                    i = ec * 2 + half
                    sl = i % 3
                    h0 = half * 2816
                    b.wait(G, sfree.get(("intd", sl)))
                    ev_ld = b.dch(f"p0id{sl}").inc(G.dma_start(p0_intd[:, sl * 2816:(sl + 1) * 2816],
                                               qd_d[ec * 128:(ec + 1) * 128, h0:h0 + 2816]), 16)
                    b.wait(S, ev_ld, sfree.get(("deqd", sl)))
                    ev_deq = cS.inc(S.activation(p0_deqd[:, sl * 2816:(sl + 1) * 2816],
                                                 p0_intd[:, sl * 2816:(sl + 1) * 2816], AF.Identity,
                                                 bias=nsd_col[:, ec:ec + 1], scale=sd_col[:, ec:ec + 1]))
                    sfree[("intd", sl)] = ev_deq
                    ev_el = []
                    for g in range(6):
                        npc = min(4, 22 - g * 4)
                        if npc <= 0:
                            break
                        rb = grot % 6
                        grot += 1
                        b.wait(T, ev_deq, bank_free[rb])
                        for kk in range(npc):
                            kh = g * 4 + kk
                            ins = T.matmul(PB[rb][:, kk * 128:(kk + 1) * 128],
                                           p0_deqd[:, sl * 2816 + kh * 128: sl * 2816 + (kh + 1) * 128],
                                           eye_sb[:], start=True, stop=True)
                        ev_tr = cT.inc(ins)
                        eng, ch = (V, cV) if g % 2 == 0 else (S, cS)
                        b.wait(eng, ev_tr, sfree.get(("outd", sl)))
                        if eng is V:
                            ev_e = ch.inc(V.tensor_copy(
                                p0_outd[:, sl * 2816 + g * 512: sl * 2816 + g * 512 + npc * 128],
                                PB[rb][:, 0:npc * 128]))
                        else:
                            ev_e = ch.inc(S.activation(
                                p0_outd[:, sl * 2816 + g * 512: sl * 2816 + g * 512 + npc * 128],
                                PB[rb][:, 0:npc * 128], AF.Copy))
                        bank_free[rb] = ev_e
                        ev_el.append(ev_e)
                    sfree[("deqd", sl)] = ev_el[-1]
                    b.wait(Y, *ev_el)
                    last = None
                    lo = 0
                    while lo < 22:
                        khg = half * 22 + lo
                        khc, k4 = khg // 4, khg % 4
                        nk = min(4 - k4, 22 - lo)
                        row0 = (es * 11 + khc) * 128
                        dstv = wdc_d[row0:row0 + 128, :].rearrange(
                            "p (k e) -> p k e", e=128)[:, 4 * k4 + ee: 4 * (k4 + nk - 1) + ee + 1:4, :]
                        srcv = p0_outd[:, sl * 2816 + lo * 128: sl * 2816 + (lo + nk) * 128] \
                            .rearrange("p (k e) -> p k e", e=128)
                        last = b.dch(f"p0od{sl}").inc(Y.dma_start(dstv, srcv), 16)
                        lo += nk
                    sfree[("outd", sl)] = last
            nc.leave_named_scope("p0d", _sc1[0], False)
            b.quiesce()
            b.quiesce(G)
            b.barrier()
            bank_free = [None] * 8

        with (
            nc.sbuf_tensor("big0", [128, KD * TT], BF16) as big0,
            nc.sbuf_tensor("big1", [128, NH * TT], BF16) as big1,
            nc.sbuf_tensor("xstg", [128, 4 * D], F32) as xstg,
            nc.sbuf_tensor("xnstg", [128, 2 * D], BF16) as xnstg,
            nc.sbuf_tensor("trash", [128, D], BF16) as trash,
            nc.sbuf_tensor("wf32", [128, 3 * D], F32) as wf32,
            nc.sbuf_tensor("wbf", [128, 2 * D], BF16) as wbf,
            nc.sbuf_tensor("wbf3", [128, 3 * D], BF16) as wbf3,
            nc.sbuf_tensor("wo_bf", [128, 2 * KD * 256], BF16) as wo_bf,
            nc.sbuf_tensor("ysq", [128, 4 * 512], BF16) as ysq,
            nc.sbuf_tensor("gs_stg", [128, 10 * 512], BF16) as gs_stg,
            nc.sbuf_tensor("bc_RQ", [128, 2 * TT], F32) as bc_RQ,
            nc.sbuf_tensor("rows", [1, 3 * TT], F32) as rows,
            nc.sbuf_tensor("smal", [128, 144], F32) as smal,
            nc.sbuf_tensor("dbg_fin_sb", [128, 64], F32) as dbg_fin_sb,
        ):
            print("SBUF remaining inside A scope:", nc.sbuf_bytes_remaining)
            a_slot_free = [None] * 4
            a_xn_free = [None] * 2
            a_pref = {}
            a_ev_cast = {}
            a_ev_cw = {}
            a_ev_mm_done = {}
            a_wo_cast = {}
            a_fs_done = {}
            a_wocw = {}
            a_sq_instr = {}
            gs_free = [None] * 10
            a_rot = [0]
            keep_barriers = bool(taps) or dbg_strict >= 1
            for t in range(n_a_tiles):
                _scA = nc.enter_named_scope(f"A{t}", False)
                bb, half = t // 2, t % 2
                r0 = bb * S_LEN + half * TT
                ev_xstg_free = a_slot_free
                ev_xn_free = a_xn_free
                for grp in range(4):
                    for s in (2 * grp, 2 * grp + 1):
                        sl = s % 2
                        evs = a_pref.pop((t, s), None)
                        if evs is None:
                            b.wait(G, ev_xstg_free[sl])
                            evs = [b.dch(f"ax{sl}c{j}").inc(G.dma_start(
                                xstg[j * 32:(j + 1) * 32, sl * D:(sl + 1) * D],
                                xf[r0 + s * 128 + j * 32: r0 + s * 128 + (j + 1) * 32,
                                   :]), 16) for j in range(4)]
                        xs = xstg[:, sl * D:(sl + 1) * D]
                        b.wait(S, *evs)
                        cS.inc(S.activation(trash[:], xs, AF.Identity,
                                            accum_out=smal[:, s:s + 1]))
                        cS.inc(S.activation(trash[:], xs, AF.Square,
                                            accum_out=smal[:, 8 + s:9 + s]))
                    ev_nb = ln_group_s(smal, grp * 2, 2, [cS.ev()])
                    for s in (2 * grp, 2 * grp + 1):
                        sl = s % 2
                        c0 = s % 2
                        xs = xstg[:, sl * D:(sl + 1) * D]
                        b.wait(S, ev_nb, ev_xn_free[c0])
                        ev_xn = cS.inc(S.activation(xnstg[:, c0 * D:(c0 + 1) * D], xs, AF.Identity,
                                                    scale=smal[:, 48 + s:49 + s],
                                                    bias=smal[:, 56 + s:57 + s]))
                        ev_xstg_free[sl] = ev_xn
                        for g in range(4):
                            rb = 4 * (s % 2) + g
                            b.wait(T, ev_xn, bank_free[rb])
                            for kk in range(4):
                                kd = g * 4 + kk
                                ins = T.matmul(PB[rb][:, kk * 128:(kk + 1) * 128],
                                               xnstg[:, c0 * D + kd * 128: c0 * D + (kd + 1) * 128],
                                               eye_sb[:], start=True, stop=True)
                            ev_tr = cT.inc(ins)
                            b.wait(V, ev_tr)
                            dst = big0[:].rearrange("p (kd t) -> p kd t", kd=KD)[
                                :, g * 4:(g + 1) * 4, s * 128:(s + 1) * 128]
                            src = PB[rb][:, 0:512].rearrange("p (kd e) -> p kd e", kd=4)
                            ev_e = cV.inc(V.tensor_copy(dst, src))
                            bank_free[rb] = ev_e
                            if g == 3:
                                ev_xn_free[c0] = ev_e
                ev_xnT = cV.ev()
                if "xnT" in tap_t and t == 0:
                    b.wait(Y, ev_xnT)
                    b.dch("tp").inc(Y.dma_start(tap_t["xnT"][:], big0[:]), 16)
                if "dbg_xstg" in tap_t and t == 0:
                    b.wait(Y, ev_xnT)
                    b.dch("tp").inc(Y.dma_start(tap_t["dbg_xstg"][:], xstg[:]), 16)
                    b.dch("tp").inc(Y.dma_start(tap_t["dbg_xnstg"][:], xnstg[:]), 16)

                ev_cast = a_ev_cast
                ev_cw = a_ev_cw
                ev_mm_done = a_ev_mm_done
                rot = [2, 3, 4]
                ev_si_hb = {}
                ev_gate_done = [None] * NH
                ev_scan_done = [None] * NH
                ev_yT_last = None
                ev_R = ev_Q = None
                ev_sq_instr = a_sq_instr

                def emit_sy():
                    for tb in range(NTB):
                        b.wait(T, ev_yT_last, bank_free[5])
                        for hb in range(NH):
                            ins = T.matmul(PB[5][0:1, 0:512], onesc_sb[:],
                                           big1[:, hb * TT + tb * 512: hb * TT + (tb + 1) * 512],
                                           start=(hb == 0), stop=(hb == NH - 1))
                        ev_sy = cT.inc(ins)
                        b.wait(V, ev_sy)
                        ev_r = cV.inc(V.tensor_copy(rows[:, tb * 512:(tb + 1) * 512], PB[5][0:1, 0:512]))
                        bank_free[5] = ev_r

                def emit_ssq():
                    for tb in range(NTB):
                        b.wait(T, bank_free[6])
                        for hb in range(NH):
                            ysl = (tb * NH + hb) % 4
                            yv = ysq[:, ysl * 512:(ysl + 1) * 512]
                            y_sl = big1[:, hb * TT + tb * 512: hb * TT + (tb + 1) * 512]
                            gsq = t * 2 * NH + tb * NH + hb
                            b.wait(V, ev_sq_instr.get(gsq - 4))
                            ev_q2 = cV.inc(V.tensor_tensor(yv, y_sl, y_sl, OP.mult))
                            b.wait(T, ev_q2)
                            ins = T.matmul(PB[6][0:1, 0:512], onesc_sb[:], yv,
                                           start=(hb == 0), stop=(hb == NH - 1))
                            ev_sq_instr[gsq] = cT.inc(ins)
                        b.wait(V, ev_sq_instr[t * 2 * NH + tb * NH + NH - 1])
                        ev_r = cV.inc(V.tensor_copy(rows[:, TT + tb * 512: TT + (tb + 1) * 512],
                                                    PB[6][0:1, 0:512]))
                        bank_free[6] = ev_r

                def emit_rows():
                    nonlocal ev_R, ev_Q
                    ev1 = cV.inc(V.tensor_scalar(rows[:, 2 * TT:3 * TT], rows[:, 0:TT], RD, None, OP.mult))
                    ev2 = cV.inc(V.tensor_scalar(rows[:, 0:TT], rows[:, TT:2 * TT], RD, None, OP.mult))
                    b.wait(V, ev2)
                    ev3 = cV.inc(V.tensor_mul(rows[:, TT:2 * TT], rows[:, 2 * TT:3 * TT],
                                              rows[:, 2 * TT:3 * TT]))
                    b.wait(V, ev3)
                    ev4 = cV.inc(V.tensor_sub(rows[:, TT:2 * TT], rows[:, TT:2 * TT], rows[:, 0:TT]))
                    b.wait(S, ev4)
                    ev5 = cS.inc(S.activation(rows[:, 0:TT], rows[:, TT:2 * TT], AF.Ln,
                                              bias=eps_col[0:1, :], scale=neg_col[0:1, :]))
                    b.wait(V, ev5)
                    evh = cV.inc(V.tensor_scalar(rows[:, 0:TT], rows[:, 0:TT], -0.5, None, OP.mult))
                    b.wait(S, evh)
                    ev_R = cS.inc(S.activation(rows[:, TT:2 * TT], rows[:, 0:TT], AF.Exp))
                    b.wait(V, ev_R)
                    ev_Q = cV.inc(V.tensor_mul(rows[:, 2 * TT:3 * TT], rows[:, 2 * TT:3 * TT],
                                               rows[:, TT:2 * TT]))

                def emit_bcast():
                    for tb in range(NTB):
                        b.wait(T, ev_R, bank_free[5])
                        ev_b1 = cT.inc(T.matmul(PB[5][:, 0:512], onesr_sb[:],
                                                rows[0:1, TT + tb * 512: TT + (tb + 1) * 512],
                                                start=True, stop=True))
                        b.wait(V, ev_b1)
                        ev_e = cV.inc(V.tensor_copy(bc_RQ[:, tb * 512:(tb + 1) * 512], PB[5][:, 0:512]))
                        bank_free[5] = ev_e
                        b.wait(T, ev_Q, bank_free[6])
                        ev_b2 = cT.inc(T.matmul(PB[6][:, 0:512], onesr_sb[:],
                                                rows[0:1, 2 * TT + tb * 512: 2 * TT + (tb + 1) * 512],
                                                start=True, stop=True))
                        b.wait(V, ev_b2)
                        ev_e = cV.inc(V.tensor_copy(bc_RQ[:, TT + tb * 512: TT + (tb + 1) * 512],
                                                    PB[6][:, 0:512]))
                        bank_free[6] = ev_e

                def emit_apply():
                    for hb in range(NH):
                        y_sl = big1[:, hb * TT:(hb + 1) * TT]
                        t1 = ysq[:, 0:1024]
                        t2 = ysq[:, 1024:2048]
                        cV.inc(V.scalar_tensor_tensor(t1, y_sl, lng_col[:, hb:hb + 1],
                                                      bc_RQ[:, 0:TT], OP.mult, OP.mult))
                        ev_o2 = cV.inc(V.scalar_tensor_tensor(t2, bc_RQ[:, TT:2 * TT],
                                                              lng_col[:, hb:hb + 1],
                                                              bcol(lnb_col[:, hb:hb + 1], TT),
                                                              OP.mult, OP.subtract))
                        b.wait(V, ev_o2)
                        cV.inc(V.tensor_tensor(y_sl, t1, t2, OP.subtract))

                def emit_gate(k):
                    for tb in range(NTB):
                        gsl = (2 * k + tb) % 10
                        st_sl = big1[:, k * TT + tb * 512: k * TT + (tb + 1) * 512]
                        gv = gs_stg[:, gsl * 512:(gsl + 1) * 512]
                        b.wait(V, ev_si_hb[(k, tb)])
                        ev_g = cV.inc(V.tensor_tensor(st_sl, gv, st_sl, OP.mult))
                        gs_free[gsl] = ev_g
                        if tb == NTB - 1:
                            ev_gate_done[k] = ev_g

                blk_i = t * 3 * NH
                for wname, wsrc in (("v", wv_d), ("q", wq_d), ("g", wg_d)):
                    for hb in range(NH):
                        if wname == "g" and hb >= 5:
                            emit_gate(hb - 5)
                        i = blk_i
                        wi3 = {"v": 1, "q": 0, "g": 2}[wname]
                        crow = (wi3 * KD + hb) * 128
                        if t == 0:
                            sl2 = i % 3
                            sl2b = i % 2
                            fv = wf32[:, sl2 * D:(sl2 + 1) * D]
                            bv = wbf[:, sl2b * D:(sl2b + 1) * D]
                            b.wait(Y, ev_cast.get(i - 3))
                            for k4 in range(4):
                                srcv = wsrc[k4 * 512:(k4 + 1) * 512, hb * 128:(hb + 1) * 128] \
                                    .rearrange("(kd p) e -> p kd e", p=128)
                                dstv = fv.rearrange("p (kd e) -> p kd e", kd=KD)[:, k4 * 4:(k4 + 1) * 4, :]
                                ev_ld = b.dch(f"aw{sl2}").inc(Y.dma_start(dstv, srcv), 16)
                            eng, ch = (S, cS) if i % 2 == 0 else (V, cV)
                            b.wait(eng, ev_ld, ev_mm_done.get(i - 2), ev_cw.get(i - 2))
                            if eng is S:
                                ev_cast[i] = ch.inc(S.activation(bv, fv, AF.Copy))
                            else:
                                ev_cast[i] = ch.inc(V.tensor_copy(bv, fv))
                            b.wait(Y, ev_cast[i])
                            ev_cw[i] = b.dch(f"acw{sl2b}").inc(
                                Y.dma_start(wac_d[crow:crow + 128, :], bv), 16)
                        else:
                            sl2b = i % 3
                            bv = wbf3[:, sl2b * D:(sl2b + 1) * D]
                            b.wait(Y, ev_mm_done.get(i - 3))
                            ev_cast[i] = b.dch(f"ab{sl2b}").inc(
                                Y.dma_start(bv, wac_d[crow:crow + 128, :]), 16)
                        for tb in range(NTB):
                            rb = rot[a_rot[0] % 3]
                            a_rot[0] += 1
                            b.wait(T, ev_cast[i], ev_xnT, bank_free[rb])
                            for kd in range(KD):
                                ins = T.matmul(PB[rb][:, 0:512], bv[:, kd * 128:(kd + 1) * 128],
                                               big0[:, kd * TT + tb * 512: kd * TT + (tb + 1) * 512],
                                               start=(kd == 0), stop=(kd == KD - 1))
                            ev_mm = cT.inc(ins)
                            st_sl = big1[:, hb * TT + tb * 512: hb * TT + (tb + 1) * 512]
                            if wname == "v":
                                if tb == 0 and half == 0:
                                    init = 0.0
                                elif tb == 0:
                                    init = carry_sb[:, hb:hb + 1]
                                else:
                                    init = big1[:, hb * TT + 511: hb * TT + 512]
                                    b.wait(V, ev_sc0)
                                b.wait(V, ev_mm)
                                ev_sc = cV.inc(V.tensor_tensor_scan(
                                    st_sl, bcol(lam_bc[:, hb:hb + 1], 512), PB[rb][:, 0:512],
                                    init, OP.mult, OP.add))
                                ev_sc0 = ev_sc
                                bank_free[rb] = ev_sc
                                if tb == NTB - 1:
                                    ev_scan_done[hb] = ev_sc
                                    if half == 0:
                                        b.wait(V, ev_sc)
                                        cV.inc(V.tensor_copy(carry_sb[:, hb:hb + 1],
                                                             big1[:, (hb + 1) * TT - 1:(hb + 1) * TT]))
                            elif wname == "q":
                                b.wait(V, ev_mm)
                                ev_y = cV.inc(V.tensor_tensor(st_sl, PB[rb][:, 0:512], st_sl, OP.mult))
                                bank_free[rb] = ev_y
                                ev_yT_last = ev_y
                            else:
                                gsl = (2 * hb + tb) % 10
                                gv = gs_stg[:, gsl * 512:(gsl + 1) * 512]
                                b.wait(S, ev_mm, gs_free[gsl])
                                ev_si = cS.inc(S.activation(gv, PB[rb][:, 0:512], AF.Silu))
                                bank_free[rb] = ev_si
                                ev_si_hb[(hb, tb)] = ev_si
                        ev_mm_done[i] = cT.ev()
                        blk_i += 1
                        if wname == "g":
                            if hb == 0:
                                emit_sy()
                            elif hb == 1:
                                emit_ssq()
                            elif hb == 2:
                                emit_rows()
                            elif hb == 3:
                                emit_bcast()
                            elif hb == 4:
                                emit_apply()

                    if wname == "v" and "statesT" in tap_t and t == 0:
                        b.wait(Y, *ev_scan_done)
                        b.dch("tp").inc(Y.dma_start(tap_t["statesT"][:], big1[:]), 16)

                for k in range(NH - 5, NH):
                    emit_gate(k)

                if "ygatedT" in tap_t and t == 0:
                    b.wait(Y, *ev_gate_done)
                    b.dch("tp").inc(Y.dma_start(tap_t["ygatedT"][:], big1[:]), 16)
                ev_wo_cast = a_wo_cast
                ev_fs_done = a_fs_done
                ev_wocw_d = a_wocw
                ev_sum_done = {}
                ev_last_casts = [ev_cast.get(3 * NH - 3), ev_cast.get(3 * NH - 2), ev_cast.get(3 * NH - 1)]
                ev_wocw = ev_wocw_d
                for fsb in range(NFS):
                    gf = t * NFS + fsb
                    osl = fsb % 2
                    if t == 0:
                        for ke in range(KD):
                            i = fsb * KD + ke
                            islot = i % 8
                            fv = wf32[:, islot * 256:(islot + 1) * 256]
                            b.wait(Y, ev_wo_cast.get(i - 8), *(ev_last_casts if i < 8 else []))
                            ev_ld = b.dch(f"ao{islot}").inc(Y.dma_start(
                                fv, wo_d[ke * 128:(ke + 1) * 128, fsb * 256:(fsb + 1) * 256]), 16)
                            eng, ch = (S, cS) if i % 2 == 0 else (V, cV)
                            wdst = wo_bf[:, osl * KD * 256 + ke * 256: osl * KD * 256 + (ke + 1) * 256]
                            b.wait(eng, ev_ld, ev_fs_done.get(gf - 2), ev_wocw.get(gf - 2))
                            if eng is S:
                                ev_wo_cast[i] = ch.inc(S.activation(wdst, fv, AF.Copy))
                            else:
                                ev_wo_cast[i] = ch.inc(V.tensor_copy(wdst, fv))
                        b.wait(Y, *[ev_wo_cast[fsb * KD + k] for k in range(KD)])
                        ev_wocw[gf] = b.dch(f"wocw{osl}").inc(Y.dma_start(
                            woc_d[fsb * 128:(fsb + 1) * 128, :],
                            wo_bf[:, osl * KD * 256:(osl + 1) * KD * 256]), 16)
                    else:
                        b.wait(Y, ev_fs_done.get(gf - 2), ev_wocw.get(gf - 2))
                        ev_l = b.dch(f"wocw{osl}").inc(Y.dma_start(
                            wo_bf[:, osl * KD * 256:(osl + 1) * KD * 256],
                            woc_d[fsb * 128:(fsb + 1) * 128, :]), 16)
                        ev_wocw[gf] = ev_l
                        for ke in range(KD):
                            ev_wo_cast[fsb * KD + ke] = ev_l
                    for s in range(NSUB):
                        rb = (fsb * NSUB + s) % 4
                        b.wait(T, bank_free[rb])
                        for ke in range(KD):
                            b.wait(T, ev_wo_cast[fsb * KD + ke], ev_gate_done[ke])
                            ins = T.matmul(PB[rb][:, 0:256],
                                           big1[:, ke * TT + s * 128: ke * TT + (s + 1) * 128],
                                           wo_bf[:, osl * KD * 256 + ke * 256: osl * KD * 256 + (ke + 1) * 256],
                                           start=(ke == 0), stop=(ke == KD - 1))
                        ev_mm = cT.inc(ins)
                        dsl = big0[:, s * 2048 + fsb * 256: s * 2048 + (fsb + 1) * 256]
                        b.wait(V, ev_mm)
                        ev_rs = cV.inc(V.reduce_sum(smal[:, 64 + s * 8 + fsb: 65 + s * 8 + fsb],
                                                    PB[rb][:, 0:256], axis=AX))
                        b.wait(S, ev_rs)
                        ev_ev = cS.inc(S.activation(dsl, PB[rb][:, 0:256], AF.Copy))
                        bank_free[rb] = [ev_ev, ev_rs]
                        ev_sum_done[(s, fsb)] = [ev_ev, ev_rs]
                    ev_fs_done[gf] = cT.ev()
                if "dbg_att" in tap_t and t == 0:
                    b.wait(Y, *[ev_sum_done[(s_, f_)] for s_ in range(NSUB) for f_ in range(NFS)])
                    b.dch("tp").inc(Y.dma_start(tap_t["dbg_att"][:], big0[:]), 16)
                ev_x2_tap = []
                ev_stage_free = a_slot_free
                if t + 1 < n_a_tiles:
                    r1 = ((t + 1) // 2) * S_LEN + ((t + 1) % 2) * TT
                    for s2 in (0, 1):
                        b.wait(G, a_slot_free[s2])
                        a_pref[(t + 1, s2)] = [b.dch(f"ax{s2}c{j}").inc(G.dma_start(
                            xstg[j * 32:(j + 1) * 32, s2 * D:(s2 + 1) * D],
                            xf[r1 + s2 * 128 + j * 32: r1 + s2 * 128 + (j + 1) * 32,
                               :]), 16) for j in range(4)]
                for s in range(NSUB):
                    for fsb in range(NFS):
                        b.wait(V, *ev_sum_done[(s, fsb)])
                    dsl_full = big0[:, s * 2048:(s + 1) * 2048]
                    cV.inc(V.scalar_tensor_tensor(trash[:], dsl_full, 1.0, dsl_full,
                                                  OP.mult, OP.mult, accum_out=smal[:, 128:129]))
                    ev_red = cV.inc(V.tensor_reduce(smal[:, 129:130], smal[:, 64 + s * 8:64 + s * 8 + 8],
                                                    AX, OP.add))
                    b.wait(V, ev_red)
                    ev_mu6 = cV.inc(V.tensor_scalar(smal[:, 130:131], smal[:, 129:130], RD, None, OP.mult))
                    ev_msq6 = cV.inc(V.tensor_scalar(smal[:, 131:132], smal[:, 128:129], RD, None, OP.mult))
                    b.wait(V, ev_msq6)
                    ev_nv = cV.inc(V.scalar_tensor_tensor(smal[:, 132:133], smal[:, 130:131],
                                                          smal[:, 130:131], smal[:, 131:132],
                                                          OP.mult, OP.subtract))
                    b.wait(S, ev_nv)
                    ev_sg = cS.inc(S.activation(smal[:, 133:134], smal[:, 132:133], AF.Ln,
                                                bias=eps_col[:], scale=neg_col[:]))
                    b.wait(V, ev_sg)
                    evh6 = cV.inc(V.tensor_scalar(smal[:, 133:134], smal[:, 133:134],
                                                  -0.5, None, OP.mult))
                    b.wait(S, evh6)
                    ev_rc6 = cS.inc(S.activation(smal[:, 134:135], smal[:, 133:134], AF.Exp))
                    b.wait(V, ev_rc6)
                    ev_nb = cV.inc(V.tensor_scalar(smal[:, 135:136], smal[:, 130:131], smal[:, 134:135],
                                                   -1.0, OP.mult, OP.mult))
                    if "dbg_fin" in tap_t and t == 0:
                        b.wait(V, ev_nb)
                        cV.inc(V.tensor_copy(dbg_fin_sb[:, s * 8:(s + 1) * 8], smal[:, 128:136]))
                    ssl = s % 2
                    slv = xstg[:, (2 + ssl) * D:(3 + ssl) * D]
                    b.wait(G, ev_stage_free[2 + ssl])
                    ev_xr = [b.dch(f"ax{2 + ssl}c{j}").inc(G.dma_start(
                        xstg[j * 32:(j + 1) * 32, (2 + ssl) * D:(3 + ssl) * D],
                        xf[r0 + s * 128 + j * 32: r0 + s * 128 + (j + 1) * 32, :]), 16)
                        for j in range(4)]
                    b.wait(V, ev_nb, *ev_xr)
                    ev_p = cV.inc(V.tensor_scalar(slv, slv, smal[:, 135:136], None, OP.add))
                    b.wait(V, ev_p)
                    ev_o = cV.inc(V.scalar_tensor_tensor(slv, dsl_full, smal[:, 134:135], slv,
                                                         OP.mult, OP.add))
                    b.wait(G, ev_o)
                    ev_w = [b.dch(f"ax{2 + ssl}c{j}").inc(G.dma_start(
                        x2_d[r0 + s * 128 + j * 32: r0 + s * 128 + (j + 1) * 32, :],
                        xstg[j * 32:(j + 1) * 32, (2 + ssl) * D:(3 + ssl) * D]), 16)
                        for j in range(4)]
                    if "x2" in tap_t and t == 0:
                        ev_w = [b.dch(f"ax{2 + ssl}c0").inc(G.dma_start(
                            tap_t["x2"][s * 128:(s + 1) * 128, :], slv), 16)]
                    ev_stage_free[2 + ssl] = ev_w
                if "dbg_fin" in tap_t and t == 0:
                    b.wait(Y, cV.ev())
                    b.dch("tp").inc(Y.dma_start(tap_t["dbg_fin"][:], dbg_fin_sb[:]), 16)
                if "dbg_smal" in tap_t and t == 0:
                    b.wait(Y, cV.ev(), cS.ev())
                    b.dch("tp").inc(Y.dma_start(tap_t["dbg_smal"][:], smal[:]), 16)
                nc.leave_named_scope(f"A{t}", _scA[0], False)
                if keep_barriers or t == n_a_tiles - 1:
                    b.quiesce()
                    b.quiesce(G)
                    b.barrier()
                    bank_free = [None] * 8

        with (
            nc.sbuf_tensor("bigh", [128, NHB * TT], BF16) as bigh,
            nc.sbuf_tensor("big2", [128, KD * TT], BF16) as big2,
            nc.sbuf_tensor("bxstg", [128, 4 * D], F32) as bxstg,
            nc.sbuf_tensor("bxnstg", [128, 2 * D], BF16) as bxnstg,
            nc.sbuf_tensor("btrash", [128, D], BF16) as btrash,
            nc.sbuf_tensor("wgu", [128, 4 * D], BF16) as wgu,
            nc.sbuf_tensor("wdr", [128, 2 * 2048], BF16) as wdr,
            nc.sbuf_tensor("bgs", [128, 2 * 512], BF16) as bgs,
            nc.sbuf_tensor("bstat", [128, 112], F32) as bstat,
        ):
            print("SBUF remaining inside B scope:", nc.sbuf_bytes_remaining)
            b_slot_free = [None] * 4
            b_xn_free = [None] * 2
            b_last_stt = [None]
            b_wld = {}
            b_wfree = {}
            b_chunk = {}
            b_chunk_free = {}
            bgs_free = [None, None]
            keep_barriers = bool(taps) or dbg_strict >= 1
            b_apply = {}
            b_xn2T_tb = {}
            b_pref = {}

            def b1_front(tt, grps):
                rr = tt * TT
                for grp in grps:
                    for s in (2 * grp, 2 * grp + 1):
                        sl = s % 2
                        evs = b_pref.pop((tt, s), None)
                        if evs is None:
                            b.wait(S, b_slot_free[sl])
                            evs = [b.dch(f"bx{sl}c{j}").inc(S.dma_start(
                                bxstg[j * 32:(j + 1) * 32, sl * D:(sl + 1) * D],
                                x2_d[rr + s * 128 + j * 32: rr + s * 128 + (j + 1) * 32,
                                     :]), 16) for j in range(4)]
                        xs = bxstg[:, sl * D:(sl + 1) * D]
                        b.wait(S, *evs)
                        cS.inc(S.activation(btrash[:], xs, AF.Identity,
                                            accum_out=bstat[:, s:s + 1]))
                        cS.inc(S.activation(btrash[:], xs, AF.Square,
                                            accum_out=bstat[:, 8 + s:9 + s]))
                    ev_nb = ln_group_s(bstat, grp * 2, 2, [cS.ev()])
                    for s in (2 * grp, 2 * grp + 1):
                        sl = s % 2
                        c0 = s % 2
                        xs = bxstg[:, sl * D:(sl + 1) * D]
                        b.wait(S, ev_nb, b_xn_free[c0])
                        ev_xn = cS.inc(S.activation(bxnstg[:, c0 * D:(c0 + 1) * D], xs, AF.Identity,
                                                    scale=bstat[:, 48 + s:49 + s],
                                                    bias=bstat[:, 56 + s:57 + s]))
                        b_slot_free[sl] = ev_xn
                        b_apply[(tt, s)] = ev_xn

            def b1_back(tt, ss):
                if 0 in ss:
                    b.wait(S, b_last_stt[0])
                for s in ss:
                    c0 = s % 2
                    eng, ch = (S, cS) if s < 4 else (V, cV)
                    for g in range(4):
                        rb = 4 * (s % 2) + g
                        b.wait(T, b_apply[(tt, s)], bank_free[rb])
                        for kk in range(4):
                            kd = g * 4 + kk
                            ins = T.matmul(PB[rb][:, kk * 128:(kk + 1) * 128],
                                           bxnstg[:, c0 * D + kd * 128: c0 * D + (kd + 1) * 128],
                                           eye_sb[:], start=True, stop=True)
                        ev_tr = cT.inc(ins)
                        b.wait(eng, ev_tr)
                        dst = big2[:].rearrange("p (kd t) -> p kd t", kd=KD)[
                            :, g * 4:(g + 1) * 4, s * 128:(s + 1) * 128]
                        src = PB[rb][:, 0:512].rearrange("p (kd e) -> p kd e", kd=4)
                        if eng is S:
                            ev_e = ch.inc(S.activation(dst, src, AF.Copy))
                        else:
                            ev_e = ch.inc(V.tensor_copy(dst, src))
                        bank_free[rb] = ev_e
                        if g == 3:
                            b_xn_free[c0] = ev_e
                    if s == 3:
                        b_xn2T_tb[(tt, 0)] = [cS.ev()]
                    if s == 7:
                        b_xn2T_tb[(tt, 1)] = [cS.ev(), cV.ev()]

            for t in range(n_b_tiles):
                _scB = nc.enter_named_scope(f"B{t}", False)
                r0 = t * TT
                for grp in range(4):
                    b1_front(t, (grp,))
                    b1_back(t, (2 * grp, 2 * grp + 1))
                if "xn2T" in tap_t and t == 0:
                    b.wait(Y, *b_xn2T_tb[(0, 1)], *b_xn2T_tb[(0, 0)])
                    b.dch("tp").inc(Y.dma_start(tap_t["xn2T"][:], big2[:]), 16)

                ev_wld = b_wld
                ev_wfree = b_wfree
                for hb in range(NHB):
                    ghb = t * NHB + hb
                    sl = hb % 2
                    for wi, cache in enumerate((wgc_d, wuc_d)):
                        b.wait(Y, ev_wfree.get((wi, ghb - 2)))
                        ev_wld[(wi, ghb)] = b.dch(f"bw{wi * 2 + sl}").inc(Y.dma_start(
                            wgu[:, (wi * 2 + sl) * D:(wi * 2 + sl + 1) * D],
                            cache[hb * 128:(hb + 1) * 128, :]), 16)
                    for tb in range(NTB):
                        i = 2 * hb + tb
                        rbg, rbu = (2 * i) % 8, (2 * i + 1) % 8
                        b.wait(T, ev_wld[(0, ghb)], *b_xn2T_tb[(t, tb)])
                        b.wait(T, bank_free[rbg])
                        for kd in range(KD):
                            ins = T.matmul(PB[rbg][:, 0:512],
                                           wgu[:, sl * D + kd * 128: sl * D + (kd + 1) * 128],
                                           big2[:, kd * TT + tb * 512: kd * TT + (tb + 1) * 512],
                                           start=(kd == 0), stop=(kd == KD - 1))
                        ev_gmm = cT.inc(ins)
                        b.wait(T, ev_wld[(1, ghb)], bank_free[rbu])
                        for kd in range(KD):
                            ins = T.matmul(PB[rbu][:, 0:512],
                                           wgu[:, (2 + sl) * D + kd * 128: (2 + sl) * D + (kd + 1) * 128],
                                           big2[:, kd * TT + tb * 512: kd * TT + (tb + 1) * 512],
                                           start=(kd == 0), stop=(kd == KD - 1))
                        ev_umm = cT.inc(ins)
                        if tb == NTB - 1:
                            ev_wfree[(0, ghb)] = ev_gmm
                            ev_wfree[(1, ghb)] = ev_umm
                        gv = bgs[:, tb * 512:(tb + 1) * 512]
                        b.wait(S, ev_gmm, bgs_free[tb])
                        ev_si = cS.inc(S.activation(gv, PB[rbg][:, 0:512], AF.Silu))
                        bank_free[rbg] = ev_si
                        b.wait(V, ev_si, ev_umm)
                        ev_h = cV.inc(V.tensor_tensor(
                            bigh[:, hb * TT + tb * 512: hb * TT + (tb + 1) * 512],
                            gv, PB[rbu][:, 0:512], OP.mult))
                        bank_free[rbu] = ev_h
                        bgs_free[tb] = ev_h
                ev_hT = cV.ev()
                if "hT" in tap_t and t == 0:
                    b.wait(Y, ev_hT)
                    b.dch("tp").inc(Y.dma_start(tap_t["hT"][:], bigh[:]), 16)

                ev_chunk = b_chunk
                ev_chunk_free = b_chunk_free
                ev_ds = {}
                for es in range(4):
                    for khc in range(11):
                        cl = es * 11 + khc
                        ci = t * 44 + cl
                        sl = ci % 2
                        b.wait(Y, ev_chunk_free.get(ci - 2))
                        ev_chunk[ci] = b.dch(f"bd{sl}").inc(Y.dma_start(
                            wdr[:, sl * 2048:(sl + 1) * 2048],
                            wdc_d[cl * 128:(cl + 1) * 128, :]), 16)
                        for k4 in range(4):
                            kh = khc * 4 + k4
                            for s in range(NSUB):
                                b.wait(T, ev_chunk[ci], ev_hT,
                                       bank_free[s] if kh == 0 else None)
                                ins = T.matmul(PB[s][:, 0:512],
                                               bigh[:, kh * TT + s * 128: kh * TT + (s + 1) * 128],
                                               wdr[:, sl * 2048 + k4 * 512: sl * 2048 + (k4 + 1) * 512],
                                               start=(kh == 0), stop=(kh == NHB - 1),
                                               skip_group_check=True)
                        ev_chunk_free[ci] = cT.inc(ins)
                    ev_es = cT.ev()
                    for s in range(NSUB):
                        dsl = big2[:, s * 2048 + es * 512: s * 2048 + (es + 1) * 512]
                        b.wait(V, ev_es)
                        ev_rs = cV.inc(V.reduce_sum(bstat[:, 64 + s * 4 + es:65 + s * 4 + es],
                                                    PB[s][:, 0:512], axis=AX))
                        b.wait(S, ev_rs)
                        ev_ev = cS.inc(S.activation(dsl, PB[s][:, 0:512], AF.Copy))
                        bank_free[s] = [ev_ev, ev_rs]
                        ev_ds[(s, es)] = [ev_ev, ev_rs]

                if t + 1 < n_b_tiles:
                    for s in (0, 1):
                        b.wait(G, b_slot_free[s])
                        b_pref[(t + 1, s)] = [b.dch(f"bx{s}c{j}").inc(G.dma_start(
                            bxstg[j * 32:(j + 1) * 32, s * D:(s + 1) * D],
                            x2_d[(t + 1) * TT + s * 128 + j * 32:
                                 (t + 1) * TT + s * 128 + (j + 1) * 32, :]), 16)
                            for j in range(4)]
                for s in range(NSUB):
                    for es in range(4):
                        b.wait(V, *ev_ds[(s, es)])
                    dsl_full = big2[:, s * 2048:(s + 1) * 2048]
                    cV.inc(V.scalar_tensor_tensor(bigh[:, 0:D], dsl_full, 1.0, dsl_full,
                                                  OP.mult, OP.mult, accum_out=bstat[:, 96:97]))
                    ev_redB = cV.inc(V.tensor_reduce(bstat[:, 97:98], bstat[:, 64 + s * 4:64 + s * 4 + 4],
                                                     AX, OP.add))
                    b.wait(V, ev_redB)
                    ev_muB = cV.inc(V.tensor_scalar(bstat[:, 98:99], bstat[:, 97:98], RD, None, OP.mult))
                    ev_msqB = cV.inc(V.tensor_scalar(bstat[:, 99:100], bstat[:, 96:97], RD, None, OP.mult))
                    b.wait(V, ev_msqB)
                    ev_nv = cV.inc(V.scalar_tensor_tensor(bstat[:, 100:101], bstat[:, 98:99],
                                                          bstat[:, 98:99], bstat[:, 99:100],
                                                          OP.mult, OP.subtract))
                    b.wait(S, ev_nv)
                    ev_sg = cS.inc(S.activation(bstat[:, 101:102], bstat[:, 100:101], AF.Ln,
                                                bias=eps_col[:], scale=neg_col[:]))
                    b.wait(S, ev_sg)
                    evhB = cS.inc(S.mul(bstat[:, 101:102], bstat[:, 101:102], -0.5))
                    b.wait(S, evhB)
                    ev_rcB = cS.inc(S.activation(bstat[:, 102:103], bstat[:, 101:102], AF.Exp))
                    b.wait(S, ev_rcB)
                    ev_t2 = cS.inc(S.mul(bstat[:, 103:104], bstat[:, 98:99], bstat[:, 102:103]))
                    b.wait(S, ev_t2)
                    ev_nb = cS.inc(S.mul(bstat[:, 103:104], bstat[:, 103:104], -1.0))
                    ssl = s % 2
                    slv = bxstg[:, (2 + ssl) * D:(3 + ssl) * D]
                    b.wait(G, b_slot_free[2 + ssl])
                    ev_xr = [b.dch(f"bx{2 + ssl}c{j}").inc(G.dma_start(
                        bxstg[j * 32:(j + 1) * 32, (2 + ssl) * D:(3 + ssl) * D],
                        x2_d[r0 + s * 128 + j * 32: r0 + s * 128 + (j + 1) * 32, :]), 16)
                        for j in range(4)]
                    b.wait(V, ev_nb, *ev_xr)
                    ev_p = cV.inc(V.tensor_scalar(slv, slv, bstat[:, 103:104], None, OP.add))
                    b.wait(V, ev_p)
                    ev_o = cV.inc(V.scalar_tensor_tensor(slv, dsl_full, bstat[:, 102:103], slv,
                                                         OP.mult, OP.add))
                    b_last_stt[0] = ev_o
                    b.wait(G, ev_o)
                    ev_w = [b.dch(f"bx{2 + ssl}c{j}").inc(G.dma_start(
                        outf[r0 + s * 128 + j * 32: r0 + s * 128 + (j + 1) * 32, :],
                        bxstg[j * 32:(j + 1) * 32, (2 + ssl) * D:(3 + ssl) * D]), 16)
                        for j in range(4)]
                    b_slot_free[2 + ssl] = ev_w

                nc.leave_named_scope(f"B{t}", _scB[0], False)
                if keep_barriers or t == n_b_tiles - 1:
                    b.quiesce()
                    b.quiesce(G)
                    b.barrier()
                    bank_free = [None] * 8

        b.quiesce()
        b.quiesce(G)

    nc.finalize()
    return nc


def kernel(**inputs):
    x = np.ascontiguousarray(inputs["L0_attn_x"])
    nc = build()
    wk = [k for k in inputs if k != "L0_attn_x"]
    in_maps = []
    for c in range(8):
        m = {k: np.ascontiguousarray(inputs[k]) for k in wk}
        m["L0_attn_x"] = np.ascontiguousarray(x[2 * c:2 * c + 2])
        in_maps.append(m)
    res = run_bass_kernel_spmd(nc, in_maps, core_ids=list(range(8)))
    out = np.concatenate([res.results[c]["out"] for c in range(8)], axis=0)
    return out.astype(np.float32)


# version 16
# speedup vs baseline: 1.2683x; 1.2683x over previous
import dataclasses
import numpy as np
import ml_dtypes

import concourse.bass as bass
import concourse.mybir as mybir
from concourse.alu_op_type import AluOpType as OP
from concourse.bass_utils import run_bass_kernel_spmd

F32 = mybir.dt.float32
BF16 = mybir.dt.bfloat16
I32 = mybir.dt.int32
AF = mybir.ActivationFunctionType
AX = mybir.AxisListType.X

D = 2048
HID = 5632
NH = 16
S_LEN = 2048
B_CORE = 2
TT = 1024
NTILES = 4
NSUB = 8
NTB = 2
KD = 16
NHB = 44
NFS = 8
EPS = 1e-5
RD = 1.0 / D


class Ch:
    def __init__(self, nc, name):
        self.sem = nc.alloc_semaphore(name)
        self.name = name
        self.n = 0

    def inc(self, instr, k=1):
        instr.then_inc(self.sem, k)
        self.n += k
        return (self, self.n)

    def ev(self):
        return (self, self.n)


class Bld:
    def __init__(self):
        nc = bass.Bass("TRN2", target_bir_lowering=False, debug=False)
        self.nc = nc
        self.T, self.V, self.S, self.G, self.Y = nc.tensor, nc.vector, nc.scalar, nc.gpsimd, nc.sync
        self.cT, self.cV, self.cS, self.cG = Ch(nc, "cT"), Ch(nc, "cV"), Ch(nc, "cS"), Ch(nc, "cG")
        self.dchs = {}
        self._wc = {}

    def dch(self, name):
        if name not in self.dchs:
            self.dchs[name] = Ch(self.nc, "q_" + name)
        return self.dchs[name]

    def quiesce(self, eng=None):
        eng = eng or self.Y
        for ch in self.dchs.values():
            self.wait(eng, ch.ev())

    def wait(self, eng, *evs):
        key = id(eng)
        flat = []
        for ev in evs:
            if isinstance(ev, list):
                flat.extend(ev)
            else:
                flat.append(ev)
        for ev in flat:
            if ev is None:
                continue
            ch, n = ev
            if n <= 0:
                continue
            if self._wc.get((key, ch.name), 0) >= n:
                continue
            eng.wait_ge(ch.sem, n)
            self._wc[(key, ch.name)] = n

    def barrier(self):
        self.nc.all_engine_barrier()
        for eng in (self.T, self.V, self.S, self.G, self.Y):
            for ch in [self.cT, self.cV, self.cS, self.cG] + list(self.dchs.values()):
                k = (id(eng), ch.name)
                self._wc[k] = max(self._wc.get(k, 0), ch.n)


def bcol(ap, n):
    return dataclasses.replace(ap, ap=[ap.ap[0], [0, n]])


def build(taps=(), n_a_tiles=NTILES, n_b_tiles=NTILES, dbg_strict=0):
    b = Bld()
    nc = b.nc
    T, V, S, G, Y = b.T, b.V, b.S, b.G, b.Y
    cT, cV, cS, cG = b.cT, b.cV, b.cS, b.cG

    x_d = nc.dram_tensor("L0_attn_x", [B_CORE, S_LEN, D], F32, kind="ExternalInput").ap()
    wq_d = nc.dram_tensor("L0_attn_Wq", [D, D], F32, kind="ExternalInput").ap()
    wv_d = nc.dram_tensor("L0_attn_Wv", [D, D], F32, kind="ExternalInput").ap()
    wg_d = nc.dram_tensor("L0_attn_Wg", [D, D], F32, kind="ExternalInput").ap()
    wo_d = nc.dram_tensor("L0_attn_Wo", [D, D], F32, kind="ExternalInput").ap()
    beta_d = nc.dram_tensor("L0_attn_beta", [NH], F32, kind="ExternalInput").ap()
    lng_d = nc.dram_tensor("L0_attn_ln_g", [D], F32, kind="ExternalInput").ap()
    lnb_d = nc.dram_tensor("L0_attn_ln_b", [D], F32, kind="ExternalInput").ap()
    qg_d = nc.dram_tensor("L0_ffn_wq_gate", [HID, D], I32, kind="ExternalInput").ap()
    sg_d = nc.dram_tensor("L0_ffn_s_gate", [HID], F32, kind="ExternalInput").ap()
    qu_d = nc.dram_tensor("L0_ffn_wq_up", [HID, D], I32, kind="ExternalInput").ap()
    su_d = nc.dram_tensor("L0_ffn_s_up", [HID], F32, kind="ExternalInput").ap()
    qd_d = nc.dram_tensor("L0_ffn_wq_down", [D, HID], I32, kind="ExternalInput").ap()
    sd_d = nc.dram_tensor("L0_ffn_s_down", [D], F32, kind="ExternalInput").ap()
    out_d = nc.dram_tensor("out", [B_CORE, S_LEN, D], F32, kind="ExternalOutput").ap()

    xf = x_d.rearrange("b s d -> (b s) d")
    outf = out_d.rearrange("b s d -> (b s) d")
    x2_d = nc.dram_tensor("x2_scr", [B_CORE * S_LEN, D], F32, kind="Internal").ap()
    xn2_d = nc.dram_tensor("xn2_scr", [B_CORE * S_LEN, D], BF16, kind="Internal").ap()
    wgc_d = nc.dram_tensor("wg_cache", [NHB * 128, KD * 128], BF16, kind="Internal").ap()
    wac_d = nc.dram_tensor("wa_cache", [3 * KD * 128, KD * 128], BF16, kind="Internal").ap()
    woc_d = nc.dram_tensor("wo_cache", [NFS * 128, KD * 256], BF16, kind="Internal").ap()
    wuc_d = nc.dram_tensor("wu_cache", [NHB * 128, KD * 128], BF16, kind="Internal").ap()
    wdc_d = nc.dram_tensor("wd_cache", [4 * 11 * 128, 4 * 512], BF16, kind="Internal").ap()

    eye_d = nc.inline_tensor(np.eye(128).astype(ml_dtypes.bfloat16), "eye_c").ap()
    onesc_d = nc.inline_tensor(np.ones((128, 1), dtype=ml_dtypes.bfloat16), "onesc_c").ap()
    onesr_d = nc.inline_tensor(np.ones((1, 128), dtype=np.float32), "onesr_c").ap()

    taps = set(taps)
    tap_t = {}
    for name, shape, dt in [
        ("dbg_att", [128, NSUB * D], BF16), ("dbg_smal", [128, 128], F32), ("dbg_fin", [128, 64], F32),
        ("dbg_xstg", [128, 4 * D], F32), ("dbg_xnstg", [128, 2 * D], BF16),
        ("xnT", [128, KD * TT], BF16), ("statesT", [128, NH * TT], BF16),
        ("ygatedT", [128, NH * TT], BF16), ("x2", [TT, D], F32),
        ("xn2T", [128, KD * TT], BF16), ("hT", [128, NHB * TT], BF16),
    ]:
        if name in taps:
            tap_t[name] = nc.dram_tensor("tap_" + name, shape, dt, kind="ExternalOutput").ap()

    eye_sb = nc.alloc_sbuf_tensor("eye_sb", [128, 128], BF16)
    onesc_sb = nc.alloc_sbuf_tensor("onesc_sb", [128, 1], BF16)
    onesr_sb = nc.alloc_sbuf_tensor("onesr_sb", [1, 128], F32)
    beta_sb = nc.alloc_sbuf_tensor("beta_sb", [1, NH], F32)
    lamr_sb = nc.alloc_sbuf_tensor("lamr_sb", [1, NH], F32)
    lam_bc = nc.alloc_sbuf_tensor("lam_bc", [128, NH], F32)
    lng_col = nc.alloc_sbuf_tensor("lng_col", [128, KD], F32)
    lnb_col = nc.alloc_sbuf_tensor("lnb_col", [128, KD], F32)
    sg_col = nc.alloc_sbuf_tensor("sg_col", [128, NHB], F32)
    su_col = nc.alloc_sbuf_tensor("su_col", [128, NHB], F32)
    sd_col = nc.alloc_sbuf_tensor("sd_col", [128, KD], F32)
    nsg_col = nc.alloc_sbuf_tensor("nsg_col", [128, NHB], F32)
    nsu_col = nc.alloc_sbuf_tensor("nsu_col", [128, NHB], F32)
    nsd_col = nc.alloc_sbuf_tensor("nsd_col", [128, KD], F32)
    carry_sb = nc.alloc_sbuf_tensor("carry_sb", [128, NH], F32)
    smal4 = nc.alloc_sbuf_tensor("smal4", [128, 64], F32)
    eps_col = nc.alloc_sbuf_tensor("eps_col", [128, 1], F32)
    neg_col = nc.alloc_sbuf_tensor("neg_col", [128, 1], F32)
    one_col = nc.alloc_sbuf_tensor("one_col", [128, 1], F32)

    PB = [nc.alloc_psum_tensor(f"pb{i}", [128, 512], F32) for i in range(8)]
    bank_free = [None] * 8

    def col_load(dst, src1d):
        return G.dma_start(dst, src1d.rearrange("(p one) -> p one", one=1))

    def ln_group_s(stat, c0, w, ev_stats):
        c1 = c0 + w
        b.wait(S, *ev_stats)
        cS.inc(S.mul(stat[:, 16 + c0:16 + c1], stat[:, c0:c1], RD))
        ev1 = cS.inc(S.mul(stat[:, 24 + c0:24 + c1], stat[:, 8 + c0:8 + c1], RD))
        b.wait(S, ev1)
        cS.inc(S.activation(stat[:, 32 + c0:32 + c1], stat[:, 16 + c0:16 + c1], AF.Square))
        ev2 = cS.inc(S.mul(stat[:, 32 + c0:32 + c1], stat[:, 32 + c0:32 + c1], -1.0))
        b.wait(S, ev2)
        ev3 = cS.inc(S.add(stat[:, 32 + c0:32 + c1], stat[:, 32 + c0:32 + c1], eps_col[:]))
        b.wait(S, ev3)
        for k in range(w):
            ev4 = cS.inc(S.activation(stat[:, 40 + c0 + k:41 + c0 + k],
                                      stat[:, 24 + c0 + k:25 + c0 + k], AF.Ln,
                                      bias=stat[:, 32 + c0 + k:33 + c0 + k]))
        b.wait(S, ev4)
        ev5 = cS.inc(S.mul(stat[:, 40 + c0:40 + c1], stat[:, 40 + c0:40 + c1], -0.5))
        b.wait(S, ev5)
        ev6 = cS.inc(S.activation(stat[:, 48 + c0:48 + c1], stat[:, 40 + c0:40 + c1], AF.Exp))
        b.wait(S, ev6)
        for k in range(w):
            ev7 = cS.inc(S.mul(stat[:, 56 + c0 + k:57 + c0 + k],
                               stat[:, 16 + c0 + k:17 + c0 + k],
                               stat[:, 48 + c0 + k:49 + c0 + k]))
        b.wait(S, ev7)
        ev_nb = cS.inc(S.mul(stat[:, 56 + c0:56 + c1], stat[:, 56 + c0:56 + c1], -1.0))
        return ev_nb

    def ln_group(stat, c0, w, ev_stats, eng=None, ch=None):
        eng = eng or V
        ch = ch or cV
        c1 = c0 + w
        b.wait(eng, *ev_stats)
        ev_mu = ch.inc(eng.tensor_scalar(stat[:, 16 + c0:16 + c1], stat[:, c0:c1], RD, None, OP.mult))
        ev_msq = ch.inc(eng.tensor_scalar(stat[:, 24 + c0:24 + c1], stat[:, 8 + c0:8 + c1],
                                          RD, None, OP.mult))
        b.wait(eng, ev_msq)
        ev_m2 = ch.inc(eng.tensor_mul(stat[:, 32 + c0:32 + c1], stat[:, 16 + c0:16 + c1],
                                      stat[:, 16 + c0:16 + c1]))
        b.wait(eng, ev_m2)
        ev_nv = ch.inc(eng.tensor_sub(stat[:, 32 + c0:32 + c1], stat[:, 32 + c0:32 + c1],
                                      stat[:, 24 + c0:24 + c1]))
        b.wait(S, ev_nv)
        ev_lg = cS.inc(S.activation(stat[:, 40 + c0:40 + c1], stat[:, 32 + c0:32 + c1],
                                    AF.Ln, bias=eps_col[:], scale=neg_col[:]))
        b.wait(eng, ev_lg)
        ev_h = ch.inc(eng.tensor_scalar(stat[:, 32 + c0:32 + c1], stat[:, 40 + c0:40 + c1],
                                        -0.5, None, OP.mult))
        b.wait(S, ev_h)
        ev_rc = cS.inc(S.activation(stat[:, 48 + c0:48 + c1], stat[:, 32 + c0:32 + c1], AF.Exp))
        b.wait(eng, ev_rc)
        ev_t = ch.inc(eng.tensor_mul(stat[:, 56 + c0:56 + c1], stat[:, 16 + c0:16 + c1],
                                     stat[:, 48 + c0:48 + c1]))
        b.wait(eng, ev_t)
        ev_nb = ch.inc(eng.tensor_scalar(stat[:, 56 + c0:56 + c1], stat[:, 56 + c0:56 + c1],
                                         -1.0, None, OP.mult))
        return ev_nb

    with nc.Block() as _blk:
        b.dch("cc").inc(G.dma_start(eye_sb[:], eye_d[:]), 16)
        b.dch("cc").inc(G.dma_start(onesc_sb[:], onesc_d[:]), 16)
        b.dch("cc").inc(G.dma_start(onesr_sb[:], onesr_d[:]), 16)
        b.dch("cc").inc(G.dma_start(beta_sb[:], beta_d.rearrange("(one h) -> one h", one=1)), 16)
        for k in range(KD):
            b.dch("cc").inc(col_load(lng_col[:, k:k + 1], lng_d[k * 128:(k + 1) * 128]), 16)
            b.dch("cc").inc(col_load(lnb_col[:, k:k + 1], lnb_d[k * 128:(k + 1) * 128]), 16)
            b.dch("cc").inc(col_load(sd_col[:, k:k + 1], sd_d[k * 128:(k + 1) * 128]), 16)
        for k in range(NHB):
            b.dch("cc").inc(col_load(sg_col[:, k:k + 1], sg_d[k * 128:(k + 1) * 128]), 16)
            b.dch("cc").inc(col_load(su_col[:, k:k + 1], su_d[k * 128:(k + 1) * 128]), 16)
        ev_cst = b.dch("cc").ev()
        b.wait(S, ev_cst)
        ev_lam = cS.inc(S.activation(lamr_sb[:], beta_sb[:], AF.Sigmoid))
        b.wait(V, ev_cst)
        cV.inc(V.tensor_scalar(nsg_col[:], sg_col[:], -128.0, None, OP.mult))
        cV.inc(V.tensor_scalar(nsu_col[:], su_col[:], -128.0, None, OP.mult))
        cV.inc(V.tensor_scalar(nsd_col[:], sd_col[:], -128.0, None, OP.mult))
        cV.inc(V.memset(eps_col[:], EPS))
        cV.inc(V.memset(neg_col[:], -1.0))
        cV.inc(V.memset(one_col[:], 1.0))
        b.wait(T, ev_lam)
        ev_t = cT.inc(T.matmul(PB[0][:, 0:NH], onesr_sb[:], lamr_sb[:], start=True, stop=True))
        b.wait(V, ev_t)
        cV.inc(V.tensor_copy(lam_bc[:], PB[0][:, 0:NH]))
        b.quiesce(G)
        b.barrier()

        with (
            nc.sbuf_tensor("p0_int", [128, 3 * D], I32) as p0_int,
            nc.sbuf_tensor("p0_intd", [128, 3 * 2816], I32) as p0_intd,
            nc.sbuf_tensor("p0_deq", [128, 3 * D], BF16) as p0_deq,
            nc.sbuf_tensor("p0_deqd", [128, 3 * 2816], BF16) as p0_deqd,
            nc.sbuf_tensor("p0_out", [128, 3 * D], BF16) as p0_out,
            nc.sbuf_tensor("p0_outd", [128, 3 * 2816], BF16) as p0_outd,
        ):
            sfree = {}
            grot = 0
            _sc0 = nc.enter_named_scope("p0gu", False)
            for wi, (src, scol, ncol, dst) in enumerate(
                    [(qg_d, sg_col, nsg_col, wgc_d), (qu_d, su_col, nsu_col, wuc_d)]):
                for hb in range(NHB):
                    i = wi * NHB + hb
                    sl = i % 3
                    b.wait(G, sfree.get(("int", sl)))
                    ev_ld = b.dch(f"p0i{sl}").inc(G.dma_start(p0_int[:, sl * D:(sl + 1) * D],
                                               src[hb * 128:(hb + 1) * 128, :]), 16)
                    b.wait(S, ev_ld, sfree.get(("deq", sl)))
                    ev_deq = cS.inc(S.activation(p0_deq[:, sl * D:(sl + 1) * D],
                                                 p0_int[:, sl * D:(sl + 1) * D], AF.Identity,
                                                 bias=ncol[:, hb:hb + 1], scale=scol[:, hb:hb + 1]))
                    sfree[("int", sl)] = ev_deq
                    ev_e = None
                    for g in range(4):
                        rb = grot % 6
                        grot += 1
                        b.wait(T, ev_deq, bank_free[rb])
                        for kk in range(4):
                            kd = g * 4 + kk
                            ins = T.matmul(PB[rb][:, kk * 128:(kk + 1) * 128],
                                           p0_deq[:, sl * D + kd * 128: sl * D + (kd + 1) * 128],
                                           eye_sb[:], start=True, stop=True)
                        ev_tr = cT.inc(ins)
                        eng, ch = (V, cV) if g % 2 == 0 else (S, cS)
                        b.wait(eng, ev_tr, sfree.get(("out", sl)))
                        if eng is V:
                            ev_e = ch.inc(V.tensor_copy(
                                p0_out[:, sl * D + g * 512: sl * D + (g + 1) * 512], PB[rb][:, 0:512]))
                        else:
                            ev_e = ch.inc(S.activation(
                                p0_out[:, sl * D + g * 512: sl * D + (g + 1) * 512], PB[rb][:, 0:512],
                                AF.Copy))
                        bank_free[rb] = ev_e
                        if g == 3:
                            ev_e4 = [bank_free[(grot - 4 + k) % 6] for k in range(4)]
                    sfree[("deq", sl)] = ev_e
                    b.wait(Y, *ev_e4)
                    sfree[("out", sl)] = b.dch(f"p0o{sl}").inc(
                        Y.dma_start(dst[hb * 128:(hb + 1) * 128, :], p0_out[:, sl * D:(sl + 1) * D]), 16)
            nc.leave_named_scope("p0gu", _sc0[0], False)
            _sc1 = nc.enter_named_scope("p0d", False)
            for ec in range(KD):
                es, ee = ec // 4, ec % 4
                for half in range(2):
                    i = ec * 2 + half
                    sl = i % 3
                    h0 = half * 2816
                    b.wait(G, sfree.get(("intd", sl)))
                    ev_ld = b.dch(f"p0id{sl}").inc(G.dma_start(p0_intd[:, sl * 2816:(sl + 1) * 2816],
                                               qd_d[ec * 128:(ec + 1) * 128, h0:h0 + 2816]), 16)
                    b.wait(S, ev_ld, sfree.get(("deqd", sl)))
                    ev_deq = cS.inc(S.activation(p0_deqd[:, sl * 2816:(sl + 1) * 2816],
                                                 p0_intd[:, sl * 2816:(sl + 1) * 2816], AF.Identity,
                                                 bias=nsd_col[:, ec:ec + 1], scale=sd_col[:, ec:ec + 1]))
                    sfree[("intd", sl)] = ev_deq
                    ev_el = []
                    for g in range(6):
                        npc = min(4, 22 - g * 4)
                        if npc <= 0:
                            break
                        rb = grot % 6
                        grot += 1
                        b.wait(T, ev_deq, bank_free[rb])
                        for kk in range(npc):
                            kh = g * 4 + kk
                            ins = T.matmul(PB[rb][:, kk * 128:(kk + 1) * 128],
                                           p0_deqd[:, sl * 2816 + kh * 128: sl * 2816 + (kh + 1) * 128],
                                           eye_sb[:], start=True, stop=True)
                        ev_tr = cT.inc(ins)
                        eng, ch = (V, cV) if g % 2 == 0 else (S, cS)
                        b.wait(eng, ev_tr, sfree.get(("outd", sl)))
                        if eng is V:
                            ev_e = ch.inc(V.tensor_copy(
                                p0_outd[:, sl * 2816 + g * 512: sl * 2816 + g * 512 + npc * 128],
                                PB[rb][:, 0:npc * 128]))
                        else:
                            ev_e = ch.inc(S.activation(
                                p0_outd[:, sl * 2816 + g * 512: sl * 2816 + g * 512 + npc * 128],
                                PB[rb][:, 0:npc * 128], AF.Copy))
                        bank_free[rb] = ev_e
                        ev_el.append(ev_e)
                    sfree[("deqd", sl)] = ev_el[-1]
                    b.wait(Y, *ev_el)
                    last = None
                    lo = 0
                    while lo < 22:
                        khg = half * 22 + lo
                        khc, k4 = khg // 4, khg % 4
                        nk = min(4 - k4, 22 - lo)
                        row0 = (es * 11 + khc) * 128
                        dstv = wdc_d[row0:row0 + 128, :].rearrange(
                            "p (k e) -> p k e", e=128)[:, 4 * k4 + ee: 4 * (k4 + nk - 1) + ee + 1:4, :]
                        srcv = p0_outd[:, sl * 2816 + lo * 128: sl * 2816 + (lo + nk) * 128] \
                            .rearrange("p (k e) -> p k e", e=128)
                        last = b.dch(f"p0od{sl}").inc(Y.dma_start(dstv, srcv), 16)
                        lo += nk
                    sfree[("outd", sl)] = last
            nc.leave_named_scope("p0d", _sc1[0], False)
            b.quiesce()
            b.quiesce(G)
            b.barrier()
            bank_free = [None] * 8

        with (
            nc.sbuf_tensor("big0", [128, KD * TT], BF16) as big0,
            nc.sbuf_tensor("big1", [128, NH * TT], BF16) as big1,
            nc.sbuf_tensor("xstg", [128, 4 * D], F32) as xstg,
            nc.sbuf_tensor("xnstg", [128, 2 * D], BF16) as xnstg,
            nc.sbuf_tensor("trash", [128, D], BF16) as trash,
            nc.sbuf_tensor("wf32", [128, 3 * D], F32) as wf32,
            nc.sbuf_tensor("wbf", [128, 2 * D], BF16) as wbf,
            nc.sbuf_tensor("wbf3", [128, 3 * D], BF16) as wbf3,
            nc.sbuf_tensor("wo_bf", [128, 2 * KD * 256], BF16) as wo_bf,
            nc.sbuf_tensor("ysq", [128, 4 * 512], BF16) as ysq,
            nc.sbuf_tensor("gs_stg", [128, 10 * 512], BF16) as gs_stg,
            nc.sbuf_tensor("bc_RQ", [128, 2 * TT], F32) as bc_RQ,
            nc.sbuf_tensor("rows", [1, 3 * TT], F32) as rows,
            nc.sbuf_tensor("smal", [128, 144], F32) as smal,
            nc.sbuf_tensor("dbg_fin_sb", [128, 64], F32) as dbg_fin_sb,
        ):
            print("SBUF remaining inside A scope:", nc.sbuf_bytes_remaining)
            a_slot_free = [None] * 4
            a_xn_free = [None] * 2
            a_ev_cast = {}
            a_ev_cw = {}
            a_ev_mm_done = {}
            a_wo_cast = {}
            a_fs_done = {}
            a_wocw = {}
            a_sq_instr = {}
            gs_free = [None] * 10
            a_rot = [0]
            keep_barriers = bool(taps) or dbg_strict >= 1
            for t in range(n_a_tiles):
                _scA = nc.enter_named_scope(f"A{t}", False)
                bb, half = t // 2, t % 2
                r0 = bb * S_LEN + half * TT
                ev_xstg_free = a_slot_free
                ev_xn_free = a_xn_free
                ev_apply = [None] * NSUB
                ev_sums = {}
                for grp in range(2):
                    for s in range(grp * 4, grp * 4 + 4):
                        sl = s % 4
                        b.wait(G, ev_xstg_free[sl])
                        ev_x = b.dch(f"ax{sl}").inc(G.dma_start(
                            xstg[:, sl * D:(sl + 1) * D],
                            xf[r0 + s * 128: r0 + (s + 1) * 128, :]), 16)
                        xs = xstg[:, sl * D:(sl + 1) * D]
                        b.wait(V, ev_x)
                        ev_sums[2 * s] = cV.inc(V.reduce_sum(smal[:, s:s + 1], xs, axis=AX))
                        b.wait(S, ev_x)
                        ev_sums[2 * s + 1] = cS.inc(S.activation(trash[:], xs, AF.Square,
                                                                 accum_out=smal[:, 8 + s:9 + s]))
                    ev_nb = ln_group(smal, grp * 4, 4, [ev_sums[k] for k in range(grp * 8, grp * 8 + 8)])
                    for s in range(grp * 4, grp * 4 + 4):
                        sl = s % 4
                        c0 = s % 2
                        xs = xstg[:, sl * D:(sl + 1) * D]
                        b.wait(S, ev_nb, ev_xn_free[c0])
                        ev_xn = cS.inc(S.activation(xnstg[:, c0 * D:(c0 + 1) * D], xs, AF.Identity,
                                                    scale=smal[:, 48 + s:49 + s],
                                                    bias=smal[:, 56 + s:57 + s]))
                        ev_xstg_free[sl] = ev_xn
                        ev_apply[s] = ev_xn
                        for g in range(4):
                            rb = 4 * (s % 2) + g
                            b.wait(T, ev_xn, bank_free[rb])
                            for kk in range(4):
                                kd = g * 4 + kk
                                ins = T.matmul(PB[rb][:, kk * 128:(kk + 1) * 128],
                                               xnstg[:, c0 * D + kd * 128: c0 * D + (kd + 1) * 128],
                                               eye_sb[:], start=True, stop=True)
                            ev_tr = cT.inc(ins)
                            b.wait(V, ev_tr)
                            dst = big0[:].rearrange("p (kd t) -> p kd t", kd=KD)[
                                :, g * 4:(g + 1) * 4, s * 128:(s + 1) * 128]
                            src = PB[rb][:, 0:512].rearrange("p (kd e) -> p kd e", kd=4)
                            ev_e = cV.inc(V.tensor_copy(dst, src))
                            bank_free[rb] = ev_e
                            if g == 3:
                                ev_xn_free[c0] = ev_e
                ev_xnT = cV.ev()
                if "xnT" in tap_t and t == 0:
                    b.wait(Y, ev_xnT)
                    b.dch("tp").inc(Y.dma_start(tap_t["xnT"][:], big0[:]), 16)
                if "dbg_xstg" in tap_t and t == 0:
                    b.wait(Y, ev_xnT)
                    b.dch("tp").inc(Y.dma_start(tap_t["dbg_xstg"][:], xstg[:]), 16)
                    b.dch("tp").inc(Y.dma_start(tap_t["dbg_xnstg"][:], xnstg[:]), 16)

                ev_cast = a_ev_cast
                ev_cw = a_ev_cw
                ev_mm_done = a_ev_mm_done
                rot = [2, 3, 4]
                ev_si_hb = {}
                ev_gate_done = [None] * NH
                ev_scan_done = [None] * NH
                ev_yT_last = None
                ev_R = ev_Q = None
                ev_sq_instr = a_sq_instr

                def emit_sy():
                    for tb in range(NTB):
                        b.wait(T, ev_yT_last, bank_free[5])
                        for hb in range(NH):
                            ins = T.matmul(PB[5][0:1, 0:512], onesc_sb[:],
                                           big1[:, hb * TT + tb * 512: hb * TT + (tb + 1) * 512],
                                           start=(hb == 0), stop=(hb == NH - 1))
                        ev_sy = cT.inc(ins)
                        b.wait(V, ev_sy)
                        ev_r = cV.inc(V.tensor_copy(rows[:, tb * 512:(tb + 1) * 512], PB[5][0:1, 0:512]))
                        bank_free[5] = ev_r

                def emit_ssq():
                    for tb in range(NTB):
                        b.wait(T, bank_free[6])
                        for hb in range(NH):
                            ysl = (tb * NH + hb) % 4
                            yv = ysq[:, ysl * 512:(ysl + 1) * 512]
                            y_sl = big1[:, hb * TT + tb * 512: hb * TT + (tb + 1) * 512]
                            gsq = t * 2 * NH + tb * NH + hb
                            b.wait(V, ev_sq_instr.get(gsq - 4))
                            ev_q2 = cV.inc(V.tensor_tensor(yv, y_sl, y_sl, OP.mult))
                            b.wait(T, ev_q2)
                            ins = T.matmul(PB[6][0:1, 0:512], onesc_sb[:], yv,
                                           start=(hb == 0), stop=(hb == NH - 1))
                            ev_sq_instr[gsq] = cT.inc(ins)
                        b.wait(V, ev_sq_instr[t * 2 * NH + tb * NH + NH - 1])
                        ev_r = cV.inc(V.tensor_copy(rows[:, TT + tb * 512: TT + (tb + 1) * 512],
                                                    PB[6][0:1, 0:512]))
                        bank_free[6] = ev_r

                def emit_rows():
                    nonlocal ev_R, ev_Q
                    ev1 = cV.inc(V.tensor_scalar(rows[:, 2 * TT:3 * TT], rows[:, 0:TT], RD, None, OP.mult))
                    ev2 = cV.inc(V.tensor_scalar(rows[:, 0:TT], rows[:, TT:2 * TT], RD, None, OP.mult))
                    b.wait(V, ev2)
                    ev3 = cV.inc(V.tensor_mul(rows[:, TT:2 * TT], rows[:, 2 * TT:3 * TT],
                                              rows[:, 2 * TT:3 * TT]))
                    b.wait(V, ev3)
                    ev4 = cV.inc(V.tensor_sub(rows[:, TT:2 * TT], rows[:, TT:2 * TT], rows[:, 0:TT]))
                    b.wait(S, ev4)
                    ev5 = cS.inc(S.activation(rows[:, 0:TT], rows[:, TT:2 * TT], AF.Ln,
                                              bias=eps_col[0:1, :], scale=neg_col[0:1, :]))
                    b.wait(V, ev5)
                    evh = cV.inc(V.tensor_scalar(rows[:, 0:TT], rows[:, 0:TT], -0.5, None, OP.mult))
                    b.wait(S, evh)
                    ev_R = cS.inc(S.activation(rows[:, TT:2 * TT], rows[:, 0:TT], AF.Exp))
                    b.wait(V, ev_R)
                    ev_Q = cV.inc(V.tensor_mul(rows[:, 2 * TT:3 * TT], rows[:, 2 * TT:3 * TT],
                                               rows[:, TT:2 * TT]))

                def emit_bcast():
                    for tb in range(NTB):
                        b.wait(T, ev_R, bank_free[5])
                        ev_b1 = cT.inc(T.matmul(PB[5][:, 0:512], onesr_sb[:],
                                                rows[0:1, TT + tb * 512: TT + (tb + 1) * 512],
                                                start=True, stop=True))
                        b.wait(V, ev_b1)
                        ev_e = cV.inc(V.tensor_copy(bc_RQ[:, tb * 512:(tb + 1) * 512], PB[5][:, 0:512]))
                        bank_free[5] = ev_e
                        b.wait(T, ev_Q, bank_free[6])
                        ev_b2 = cT.inc(T.matmul(PB[6][:, 0:512], onesr_sb[:],
                                                rows[0:1, 2 * TT + tb * 512: 2 * TT + (tb + 1) * 512],
                                                start=True, stop=True))
                        b.wait(V, ev_b2)
                        ev_e = cV.inc(V.tensor_copy(bc_RQ[:, TT + tb * 512: TT + (tb + 1) * 512],
                                                    PB[6][:, 0:512]))
                        bank_free[6] = ev_e

                def emit_apply():
                    for hb in range(NH):
                        y_sl = big1[:, hb * TT:(hb + 1) * TT]
                        t1 = ysq[:, 0:1024]
                        t2 = ysq[:, 1024:2048]
                        cV.inc(V.scalar_tensor_tensor(t1, y_sl, lng_col[:, hb:hb + 1],
                                                      bc_RQ[:, 0:TT], OP.mult, OP.mult))
                        ev_o2 = cV.inc(V.scalar_tensor_tensor(t2, bc_RQ[:, TT:2 * TT],
                                                              lng_col[:, hb:hb + 1],
                                                              bcol(lnb_col[:, hb:hb + 1], TT),
                                                              OP.mult, OP.subtract))
                        b.wait(V, ev_o2)
                        cV.inc(V.tensor_tensor(y_sl, t1, t2, OP.subtract))

                def emit_gate(k):
                    for tb in range(NTB):
                        gsl = (2 * k + tb) % 10
                        st_sl = big1[:, k * TT + tb * 512: k * TT + (tb + 1) * 512]
                        gv = gs_stg[:, gsl * 512:(gsl + 1) * 512]
                        b.wait(V, ev_si_hb[(k, tb)])
                        ev_g = cV.inc(V.tensor_tensor(st_sl, gv, st_sl, OP.mult))
                        gs_free[gsl] = ev_g
                        if tb == NTB - 1:
                            ev_gate_done[k] = ev_g

                blk_i = t * 3 * NH
                for wname, wsrc in (("v", wv_d), ("q", wq_d), ("g", wg_d)):
                    for hb in range(NH):
                        if wname == "g" and hb >= 5:
                            emit_gate(hb - 5)
                        i = blk_i
                        wi3 = {"v": 1, "q": 0, "g": 2}[wname]
                        crow = (wi3 * KD + hb) * 128
                        if t == 0:
                            sl2 = i % 3
                            sl2b = i % 2
                            fv = wf32[:, sl2 * D:(sl2 + 1) * D]
                            bv = wbf[:, sl2b * D:(sl2b + 1) * D]
                            b.wait(Y, ev_cast.get(i - 3))
                            for k4 in range(4):
                                srcv = wsrc[k4 * 512:(k4 + 1) * 512, hb * 128:(hb + 1) * 128] \
                                    .rearrange("(kd p) e -> p kd e", p=128)
                                dstv = fv.rearrange("p (kd e) -> p kd e", kd=KD)[:, k4 * 4:(k4 + 1) * 4, :]
                                ev_ld = b.dch(f"aw{sl2}").inc(Y.dma_start(dstv, srcv), 16)
                            eng, ch = (S, cS) if i % 2 == 0 else (V, cV)
                            b.wait(eng, ev_ld, ev_mm_done.get(i - 2), ev_cw.get(i - 2))
                            if eng is S:
                                ev_cast[i] = ch.inc(S.activation(bv, fv, AF.Copy))
                            else:
                                ev_cast[i] = ch.inc(V.tensor_copy(bv, fv))
                            b.wait(Y, ev_cast[i])
                            ev_cw[i] = b.dch(f"acw{sl2b}").inc(
                                Y.dma_start(wac_d[crow:crow + 128, :], bv), 16)
                        else:
                            sl2b = i % 3
                            bv = wbf3[:, sl2b * D:(sl2b + 1) * D]
                            b.wait(Y, ev_mm_done.get(i - 3))
                            ev_cast[i] = b.dch(f"ab{sl2b}").inc(
                                Y.dma_start(bv, wac_d[crow:crow + 128, :]), 16)
                        for tb in range(NTB):
                            rb = rot[a_rot[0] % 3]
                            a_rot[0] += 1
                            b.wait(T, ev_cast[i], ev_xnT, bank_free[rb])
                            for kd in range(KD):
                                ins = T.matmul(PB[rb][:, 0:512], bv[:, kd * 128:(kd + 1) * 128],
                                               big0[:, kd * TT + tb * 512: kd * TT + (tb + 1) * 512],
                                               start=(kd == 0), stop=(kd == KD - 1))
                            ev_mm = cT.inc(ins)
                            st_sl = big1[:, hb * TT + tb * 512: hb * TT + (tb + 1) * 512]
                            if wname == "v":
                                if tb == 0 and half == 0:
                                    init = 0.0
                                elif tb == 0:
                                    init = carry_sb[:, hb:hb + 1]
                                else:
                                    init = big1[:, hb * TT + 511: hb * TT + 512]
                                    b.wait(V, ev_sc0)
                                b.wait(V, ev_mm)
                                ev_sc = cV.inc(V.tensor_tensor_scan(
                                    st_sl, bcol(lam_bc[:, hb:hb + 1], 512), PB[rb][:, 0:512],
                                    init, OP.mult, OP.add))
                                ev_sc0 = ev_sc
                                bank_free[rb] = ev_sc
                                if tb == NTB - 1:
                                    ev_scan_done[hb] = ev_sc
                                    if half == 0:
                                        b.wait(V, ev_sc)
                                        cV.inc(V.tensor_copy(carry_sb[:, hb:hb + 1],
                                                             big1[:, (hb + 1) * TT - 1:(hb + 1) * TT]))
                            elif wname == "q":
                                b.wait(V, ev_mm)
                                ev_y = cV.inc(V.tensor_tensor(st_sl, PB[rb][:, 0:512], st_sl, OP.mult))
                                bank_free[rb] = ev_y
                                ev_yT_last = ev_y
                            else:
                                gsl = (2 * hb + tb) % 10
                                gv = gs_stg[:, gsl * 512:(gsl + 1) * 512]
                                b.wait(S, ev_mm, gs_free[gsl])
                                ev_si = cS.inc(S.activation(gv, PB[rb][:, 0:512], AF.Silu))
                                bank_free[rb] = ev_si
                                ev_si_hb[(hb, tb)] = ev_si
                        ev_mm_done[i] = cT.ev()
                        blk_i += 1
                        if wname == "g":
                            if hb == 0:
                                emit_sy()
                            elif hb == 1:
                                emit_ssq()
                            elif hb == 2:
                                emit_rows()
                            elif hb == 3:
                                emit_bcast()
                            elif hb == 4:
                                emit_apply()

                    if wname == "v" and "statesT" in tap_t and t == 0:
                        b.wait(Y, *ev_scan_done)
                        b.dch("tp").inc(Y.dma_start(tap_t["statesT"][:], big1[:]), 16)

                for k in range(NH - 5, NH):
                    emit_gate(k)

                if "ygatedT" in tap_t and t == 0:
                    b.wait(Y, *ev_gate_done)
                    b.dch("tp").inc(Y.dma_start(tap_t["ygatedT"][:], big1[:]), 16)
                ev_wo_cast = a_wo_cast
                ev_fs_done = a_fs_done
                ev_wocw_d = a_wocw
                ev_sum_done = {}
                ev_last_casts = [ev_cast.get(3 * NH - 3), ev_cast.get(3 * NH - 2), ev_cast.get(3 * NH - 1)]
                ev_wocw = ev_wocw_d
                for fsb in range(NFS):
                    gf = t * NFS + fsb
                    osl = fsb % 2
                    if t == 0:
                        for ke in range(KD):
                            i = fsb * KD + ke
                            islot = i % 8
                            fv = wf32[:, islot * 256:(islot + 1) * 256]
                            b.wait(Y, ev_wo_cast.get(i - 8), *(ev_last_casts if i < 8 else []))
                            ev_ld = b.dch(f"ao{islot}").inc(Y.dma_start(
                                fv, wo_d[ke * 128:(ke + 1) * 128, fsb * 256:(fsb + 1) * 256]), 16)
                            eng, ch = (S, cS) if i % 2 == 0 else (V, cV)
                            wdst = wo_bf[:, osl * KD * 256 + ke * 256: osl * KD * 256 + (ke + 1) * 256]
                            b.wait(eng, ev_ld, ev_fs_done.get(gf - 2), ev_wocw.get(gf - 2))
                            if eng is S:
                                ev_wo_cast[i] = ch.inc(S.activation(wdst, fv, AF.Copy))
                            else:
                                ev_wo_cast[i] = ch.inc(V.tensor_copy(wdst, fv))
                        b.wait(Y, *[ev_wo_cast[fsb * KD + k] for k in range(KD)])
                        ev_wocw[gf] = b.dch(f"wocw{osl}").inc(Y.dma_start(
                            woc_d[fsb * 128:(fsb + 1) * 128, :],
                            wo_bf[:, osl * KD * 256:(osl + 1) * KD * 256]), 16)
                    else:
                        b.wait(Y, ev_fs_done.get(gf - 2), ev_wocw.get(gf - 2))
                        ev_l = b.dch(f"wocw{osl}").inc(Y.dma_start(
                            wo_bf[:, osl * KD * 256:(osl + 1) * KD * 256],
                            woc_d[fsb * 128:(fsb + 1) * 128, :]), 16)
                        ev_wocw[gf] = ev_l
                        for ke in range(KD):
                            ev_wo_cast[fsb * KD + ke] = ev_l
                    for s in range(NSUB):
                        rb = (fsb * NSUB + s) % 4
                        b.wait(T, bank_free[rb])
                        for ke in range(KD):
                            b.wait(T, ev_wo_cast[fsb * KD + ke], ev_gate_done[ke])
                            ins = T.matmul(PB[rb][:, 0:256],
                                           big1[:, ke * TT + s * 128: ke * TT + (s + 1) * 128],
                                           wo_bf[:, osl * KD * 256 + ke * 256: osl * KD * 256 + (ke + 1) * 256],
                                           start=(ke == 0), stop=(ke == KD - 1))
                        ev_mm = cT.inc(ins)
                        dsl = big0[:, s * 2048 + fsb * 256: s * 2048 + (fsb + 1) * 256]
                        b.wait(V, ev_mm)
                        ev_rs = cV.inc(V.reduce_sum(smal[:, 64 + s * 8 + fsb: 65 + s * 8 + fsb],
                                                    PB[rb][:, 0:256], axis=AX))
                        b.wait(S, ev_rs)
                        ev_ev = cS.inc(S.activation(dsl, PB[rb][:, 0:256], AF.Copy))
                        bank_free[rb] = [ev_ev, ev_rs]
                        ev_sum_done[(s, fsb)] = [ev_ev, ev_rs]
                    ev_fs_done[gf] = cT.ev()
                if "dbg_att" in tap_t and t == 0:
                    b.wait(Y, *[ev_sum_done[(s_, f_)] for s_ in range(NSUB) for f_ in range(NFS)])
                    b.dch("tp").inc(Y.dma_start(tap_t["dbg_att"][:], big0[:]), 16)
                ev_x2_tap = []
                ev_stage_free = a_slot_free
                for s in range(NSUB):
                    for fsb in range(NFS):
                        b.wait(V, *ev_sum_done[(s, fsb)])
                    dsl_full = big0[:, s * 2048:(s + 1) * 2048]
                    cV.inc(V.scalar_tensor_tensor(trash[:], dsl_full, 1.0, dsl_full,
                                                  OP.mult, OP.mult, accum_out=smal[:, 128:129]))
                    ev_red = cV.inc(V.tensor_reduce(smal[:, 129:130], smal[:, 64 + s * 8:64 + s * 8 + 8],
                                                    AX, OP.add))
                    b.wait(V, ev_red)
                    ev_mu6 = cV.inc(V.tensor_scalar(smal[:, 130:131], smal[:, 129:130], RD, None, OP.mult))
                    ev_msq6 = cV.inc(V.tensor_scalar(smal[:, 131:132], smal[:, 128:129], RD, None, OP.mult))
                    b.wait(V, ev_msq6)
                    ev_nv = cV.inc(V.scalar_tensor_tensor(smal[:, 132:133], smal[:, 130:131],
                                                          smal[:, 130:131], smal[:, 131:132],
                                                          OP.mult, OP.subtract))
                    b.wait(S, ev_nv)
                    ev_sg = cS.inc(S.activation(smal[:, 133:134], smal[:, 132:133], AF.Ln,
                                                bias=eps_col[:], scale=neg_col[:]))
                    b.wait(V, ev_sg)
                    evh6 = cV.inc(V.tensor_scalar(smal[:, 133:134], smal[:, 133:134],
                                                  -0.5, None, OP.mult))
                    b.wait(S, evh6)
                    ev_rc6 = cS.inc(S.activation(smal[:, 134:135], smal[:, 133:134], AF.Exp))
                    b.wait(V, ev_rc6)
                    ev_nb = cV.inc(V.tensor_scalar(smal[:, 135:136], smal[:, 130:131], smal[:, 134:135],
                                                   -1.0, OP.mult, OP.mult))
                    if "dbg_fin" in tap_t and t == 0:
                        b.wait(V, ev_nb)
                        cV.inc(V.tensor_copy(dbg_fin_sb[:, s * 8:(s + 1) * 8], smal[:, 128:136]))
                    ssl = s % 2
                    slv = xstg[:, (2 + ssl) * D:(3 + ssl) * D]
                    b.wait(G, ev_stage_free[2 + ssl])
                    ev_xr = b.dch(f"ax{2 + ssl}").inc(G.dma_start(
                        slv, xf[r0 + s * 128: r0 + (s + 1) * 128, :]), 16)
                    b.wait(V, ev_nb, ev_xr)
                    ev_p = cV.inc(V.tensor_scalar(slv, slv, smal[:, 135:136], None, OP.add))
                    b.wait(V, ev_p)
                    ev_o = cV.inc(V.scalar_tensor_tensor(slv, dsl_full, smal[:, 134:135], slv,
                                                         OP.mult, OP.add))
                    b.wait(G, ev_o)
                    ev_w = b.dch(f"ax{2 + ssl}").inc(G.dma_start(
                        x2_d[r0 + s * 128: r0 + (s + 1) * 128, :], slv), 16)
                    if "x2" in tap_t and t == 0:
                        ev_w = b.dch(f"ax{2 + ssl}").inc(G.dma_start(
                            tap_t["x2"][s * 128:(s + 1) * 128, :], slv), 16)
                    ev_su = cV.inc(V.reduce_sum(smal4[:, s:s + 1], slv, axis=AX))
                    ev_sq = cV.inc(V.scalar_tensor_tensor(trash[:], slv, 1.0, slv,
                                                          OP.mult, OP.mult,
                                                          accum_out=smal4[:, 8 + s:9 + s]))
                    ev_nb4 = ln_group_s(smal4, s, 1, [ev_su, ev_sq])
                    a6sl = s % 2
                    b.wait(S, ev_nb4, a_xn_free[a6sl])
                    ev_ap4 = cS.inc(S.activation(xnstg[:, a6sl * D:(a6sl + 1) * D], slv,
                                                 AF.Identity, scale=smal4[:, 48 + s:49 + s],
                                                 bias=smal4[:, 56 + s:57 + s]))
                    b.wait(Y, ev_ap4)
                    ev_wx = b.dch(f"xnw{a6sl}").inc(Y.dma_start(
                        xn2_d[r0 + s * 128: r0 + (s + 1) * 128, :],
                        xnstg[:, a6sl * D:(a6sl + 1) * D]), 16)
                    a_xn_free[a6sl] = ev_wx
                    ev_stage_free[2 + ssl] = [ev_w, ev_ap4]
                if "dbg_fin" in tap_t and t == 0:
                    b.wait(Y, cV.ev())
                    b.dch("tp").inc(Y.dma_start(tap_t["dbg_fin"][:], dbg_fin_sb[:]), 16)
                if "dbg_smal" in tap_t and t == 0:
                    b.wait(Y, cV.ev(), cS.ev())
                    b.dch("tp").inc(Y.dma_start(tap_t["dbg_smal"][:], smal[:]), 16)
                nc.leave_named_scope(f"A{t}", _scA[0], False)
                if keep_barriers or t == n_a_tiles - 1:
                    b.quiesce()
                    b.quiesce(G)
                    b.barrier()
                    bank_free = [None] * 8

        with (
            nc.sbuf_tensor("bigh", [128, NHB * TT], BF16) as bigh,
            nc.sbuf_tensor("big2", [128, KD * TT], BF16) as big2,
            nc.sbuf_tensor("bxstg", [128, 4 * D], F32) as bxstg,
            nc.sbuf_tensor("bxnstg", [128, 4 * D], BF16) as bxnstg,
            nc.sbuf_tensor("btrash", [128, D], BF16) as btrash,
            nc.sbuf_tensor("wgu", [128, 4 * D], BF16) as wgu,
            nc.sbuf_tensor("wdr", [128, 2 * 2048], BF16) as wdr,
            nc.sbuf_tensor("bgs", [128, 2 * 512], BF16) as bgs,
            nc.sbuf_tensor("bstat", [128, 112], F32) as bstat,
        ):
            print("SBUF remaining inside B scope:", nc.sbuf_bytes_remaining)
            b_slot_free = [None] * 4
            b_xn_free = [None] * 4
            b_last_stt = [None]
            b_wld = {}
            b_wfree = {}
            b_chunk = {}
            b_chunk_free = {}
            bgs_free = [None, None]
            keep_barriers = bool(taps) or dbg_strict >= 1
            b_apply = {}
            b_xn2T_tb = {}
            b_pref = {}

            def b1_ld(tt, s):
                sl4 = s % 4
                ev = b_pref.pop((tt, s), None)
                if ev is None:
                    b.wait(S, b_xn_free[sl4])
                    ev = b.dch(f"bn{sl4}").inc(S.dma_start(
                        bxnstg[:, sl4 * D:(sl4 + 1) * D],
                        xn2_d[tt * TT + s * 128: tt * TT + (s + 1) * 128, :]), 16)
                b_apply[(tt, s)] = ev

            def b1_back(tt, ss):
                if 0 in ss:
                    b.wait(S, b_last_stt[0])
                for s in ss:
                    c0 = s % 4
                    eng, ch = (S, cS) if s < 4 else (V, cV)
                    for g in range(4):
                        rb = 4 * (s % 2) + g
                        b.wait(T, b_apply[(tt, s)], bank_free[rb])
                        for kk in range(4):
                            kd = g * 4 + kk
                            ins = T.matmul(PB[rb][:, kk * 128:(kk + 1) * 128],
                                           bxnstg[:, c0 * D + kd * 128: c0 * D + (kd + 1) * 128],
                                           eye_sb[:], start=True, stop=True)
                        ev_tr = cT.inc(ins)
                        b.wait(eng, ev_tr)
                        dst = big2[:].rearrange("p (kd t) -> p kd t", kd=KD)[
                            :, g * 4:(g + 1) * 4, s * 128:(s + 1) * 128]
                        src = PB[rb][:, 0:512].rearrange("p (kd e) -> p kd e", kd=4)
                        if eng is S:
                            ev_e = ch.inc(S.activation(dst, src, AF.Copy))
                        else:
                            ev_e = ch.inc(V.tensor_copy(dst, src))
                        bank_free[rb] = ev_e
                        if g == 3:
                            b_xn_free[c0] = ev_e
                    if s == 3:
                        b_xn2T_tb[(tt, 0)] = [cS.ev()]
                    if s == 7:
                        b_xn2T_tb[(tt, 1)] = [cS.ev(), cV.ev()]

            for t in range(n_b_tiles):
                _scB = nc.enter_named_scope(f"B{t}", False)
                r0 = t * TT
                for s in range(4):
                    b1_ld(t, s)
                for s in range(NSUB):
                    b1_back(t, (s,))
                    if s + 4 < NSUB:
                        b1_ld(t, s + 4)
                if "xn2T" in tap_t and t == 0:
                    b.wait(Y, *b_xn2T_tb[(0, 1)], *b_xn2T_tb[(0, 0)])
                    b.dch("tp").inc(Y.dma_start(tap_t["xn2T"][:], big2[:]), 16)

                ev_wld = b_wld
                ev_wfree = b_wfree
                for hb in range(NHB):
                    ghb = t * NHB + hb
                    sl = hb % 2
                    for wi, cache in enumerate((wgc_d, wuc_d)):
                        b.wait(Y, ev_wfree.get((wi, ghb - 2)))
                        ev_wld[(wi, ghb)] = b.dch(f"bw{wi * 2 + sl}").inc(Y.dma_start(
                            wgu[:, (wi * 2 + sl) * D:(wi * 2 + sl + 1) * D],
                            cache[hb * 128:(hb + 1) * 128, :]), 16)
                    for tb in range(NTB):
                        i = 2 * hb + tb
                        rbg, rbu = (2 * i) % 8, (2 * i + 1) % 8
                        b.wait(T, ev_wld[(0, ghb)], *b_xn2T_tb[(t, tb)])
                        b.wait(T, bank_free[rbg])
                        for kd in range(KD):
                            ins = T.matmul(PB[rbg][:, 0:512],
                                           wgu[:, sl * D + kd * 128: sl * D + (kd + 1) * 128],
                                           big2[:, kd * TT + tb * 512: kd * TT + (tb + 1) * 512],
                                           start=(kd == 0), stop=(kd == KD - 1))
                        ev_gmm = cT.inc(ins)
                        b.wait(T, ev_wld[(1, ghb)], bank_free[rbu])
                        for kd in range(KD):
                            ins = T.matmul(PB[rbu][:, 0:512],
                                           wgu[:, (2 + sl) * D + kd * 128: (2 + sl) * D + (kd + 1) * 128],
                                           big2[:, kd * TT + tb * 512: kd * TT + (tb + 1) * 512],
                                           start=(kd == 0), stop=(kd == KD - 1))
                        ev_umm = cT.inc(ins)
                        if tb == NTB - 1:
                            ev_wfree[(0, ghb)] = ev_gmm
                            ev_wfree[(1, ghb)] = ev_umm
                        gv = bgs[:, tb * 512:(tb + 1) * 512]
                        b.wait(S, ev_gmm, bgs_free[tb])
                        ev_si = cS.inc(S.activation(gv, PB[rbg][:, 0:512], AF.Silu))
                        bank_free[rbg] = ev_si
                        b.wait(V, ev_si, ev_umm)
                        ev_h = cV.inc(V.tensor_tensor(
                            bigh[:, hb * TT + tb * 512: hb * TT + (tb + 1) * 512],
                            gv, PB[rbu][:, 0:512], OP.mult))
                        bank_free[rbu] = ev_h
                        bgs_free[tb] = ev_h
                ev_hT = cV.ev()
                if "hT" in tap_t and t == 0:
                    b.wait(Y, ev_hT)
                    b.dch("tp").inc(Y.dma_start(tap_t["hT"][:], bigh[:]), 16)

                ev_chunk = b_chunk
                ev_chunk_free = b_chunk_free
                ev_ds = {}
                for es in range(4):
                    for khc in range(11):
                        cl = es * 11 + khc
                        ci = t * 44 + cl
                        sl = ci % 2
                        b.wait(Y, ev_chunk_free.get(ci - 2))
                        ev_chunk[ci] = b.dch(f"bd{sl}").inc(Y.dma_start(
                            wdr[:, sl * 2048:(sl + 1) * 2048],
                            wdc_d[cl * 128:(cl + 1) * 128, :]), 16)
                        for k4 in range(4):
                            kh = khc * 4 + k4
                            for s in range(NSUB):
                                b.wait(T, ev_chunk[ci], ev_hT,
                                       bank_free[s] if kh == 0 else None)
                                ins = T.matmul(PB[s][:, 0:512],
                                               bigh[:, kh * TT + s * 128: kh * TT + (s + 1) * 128],
                                               wdr[:, sl * 2048 + k4 * 512: sl * 2048 + (k4 + 1) * 512],
                                               start=(kh == 0), stop=(kh == NHB - 1),
                                               skip_group_check=True)
                        ev_chunk_free[ci] = cT.inc(ins)
                    ev_es = cT.ev()
                    for s in range(NSUB):
                        dsl = big2[:, s * 2048 + es * 512: s * 2048 + (es + 1) * 512]
                        b.wait(V, ev_es)
                        ev_rs = cV.inc(V.reduce_sum(bstat[:, 64 + s * 4 + es:65 + s * 4 + es],
                                                    PB[s][:, 0:512], axis=AX))
                        b.wait(S, ev_rs)
                        ev_ev = cS.inc(S.activation(dsl, PB[s][:, 0:512], AF.Copy))
                        bank_free[s] = [ev_ev, ev_rs]
                        ev_ds[(s, es)] = [ev_ev, ev_rs]

                if t + 1 < n_b_tiles:
                    for s2 in range(4):
                        b.wait(G, b_xn_free[s2])
                        b_pref[(t + 1, s2)] = b.dch(f"bn{s2}").inc(G.dma_start(
                            bxnstg[:, s2 * D:(s2 + 1) * D],
                            xn2_d[(t + 1) * TT + s2 * 128:
                                  (t + 1) * TT + (s2 + 1) * 128, :]), 16)
                for s in range(NSUB):
                    for es in range(4):
                        b.wait(V, *ev_ds[(s, es)])
                    dsl_full = big2[:, s * 2048:(s + 1) * 2048]
                    cV.inc(V.scalar_tensor_tensor(bigh[:, 0:D], dsl_full, 1.0, dsl_full,
                                                  OP.mult, OP.mult, accum_out=bstat[:, 96:97]))
                    ev_redB = cV.inc(V.tensor_reduce(bstat[:, 97:98], bstat[:, 64 + s * 4:64 + s * 4 + 4],
                                                     AX, OP.add))
                    b.wait(V, ev_redB)
                    ev_muB = cV.inc(V.tensor_scalar(bstat[:, 98:99], bstat[:, 97:98], RD, None, OP.mult))
                    ev_msqB = cV.inc(V.tensor_scalar(bstat[:, 99:100], bstat[:, 96:97], RD, None, OP.mult))
                    b.wait(V, ev_msqB)
                    ev_nv = cV.inc(V.scalar_tensor_tensor(bstat[:, 100:101], bstat[:, 98:99],
                                                          bstat[:, 98:99], bstat[:, 99:100],
                                                          OP.mult, OP.subtract))
                    b.wait(S, ev_nv)
                    ev_sg = cS.inc(S.activation(bstat[:, 101:102], bstat[:, 100:101], AF.Ln,
                                                bias=eps_col[:], scale=neg_col[:]))
                    b.wait(S, ev_sg)
                    evhB = cS.inc(S.mul(bstat[:, 101:102], bstat[:, 101:102], -0.5))
                    b.wait(S, evhB)
                    ev_rcB = cS.inc(S.activation(bstat[:, 102:103], bstat[:, 101:102], AF.Exp))
                    b.wait(S, ev_rcB)
                    ev_t2 = cS.inc(S.mul(bstat[:, 103:104], bstat[:, 98:99], bstat[:, 102:103]))
                    b.wait(S, ev_t2)
                    ev_nb = cS.inc(S.mul(bstat[:, 103:104], bstat[:, 103:104], -1.0))
                    ssl = s % 2
                    slv = bxstg[:, (2 + ssl) * D:(3 + ssl) * D]
                    b.wait(G, b_slot_free[2 + ssl])
                    ev_xr = b.dch(f"bx{2 + ssl}").inc(G.dma_start(
                        slv, x2_d[r0 + s * 128: r0 + (s + 1) * 128, :]), 16)
                    b.wait(V, ev_nb, ev_xr)
                    ev_p = cV.inc(V.tensor_scalar(slv, slv, bstat[:, 103:104], None, OP.add))
                    b.wait(V, ev_p)
                    ev_o = cV.inc(V.scalar_tensor_tensor(slv, dsl_full, bstat[:, 102:103], slv,
                                                         OP.mult, OP.add))
                    b_last_stt[0] = ev_o
                    b.wait(G, ev_o)
                    ev_w = b.dch(f"bx{2 + ssl}").inc(G.dma_start(
                        outf[r0 + s * 128: r0 + (s + 1) * 128, :], slv), 16)
                    b_slot_free[2 + ssl] = ev_w

                nc.leave_named_scope(f"B{t}", _scB[0], False)
                if keep_barriers or t == n_b_tiles - 1:
                    b.quiesce()
                    b.quiesce(G)
                    b.barrier()
                    bank_free = [None] * 8

        b.quiesce()
        b.quiesce(G)

    nc.finalize()
    return nc


def kernel(**inputs):
    x = np.ascontiguousarray(inputs["L0_attn_x"])
    nc = build()
    wk = [k for k in inputs if k != "L0_attn_x"]
    in_maps = []
    for c in range(8):
        m = {k: np.ascontiguousarray(inputs[k]) for k in wk}
        m["L0_attn_x"] = np.ascontiguousarray(x[2 * c:2 * c + 2])
        in_maps.append(m)
    res = run_bass_kernel_spmd(nc, in_maps, core_ids=list(range(8)))
    out = np.concatenate([res.results[c]["out"] for c in range(8)], axis=0)
    return out.astype(np.float32)


# version 17 (final)
# speedup vs baseline: 1.2719x; 1.0028x over previous
import dataclasses
import numpy as np
import ml_dtypes

import concourse.bass as bass
import concourse.mybir as mybir
from concourse.alu_op_type import AluOpType as OP
from concourse.bass_utils import run_bass_kernel_spmd

F32 = mybir.dt.float32
BF16 = mybir.dt.bfloat16
I32 = mybir.dt.int32
AF = mybir.ActivationFunctionType
AX = mybir.AxisListType.X

D = 2048
HID = 5632
NH = 16
S_LEN = 2048
B_CORE = 2
TT = 1024
NTILES = 4
NSUB = 8
NTB = 2
KD = 16
NHB = 44
NFS = 8
EPS = 1e-5
RD = 1.0 / D


class Ch:
    def __init__(self, nc, name):
        self.sem = nc.alloc_semaphore(name)
        self.name = name
        self.n = 0

    def inc(self, instr, k=1):
        instr.then_inc(self.sem, k)
        self.n += k
        return (self, self.n)

    def ev(self):
        return (self, self.n)


class Bld:
    def __init__(self):
        nc = bass.Bass("TRN2", target_bir_lowering=False, debug=False)
        self.nc = nc
        self.T, self.V, self.S, self.G, self.Y = nc.tensor, nc.vector, nc.scalar, nc.gpsimd, nc.sync
        self.cT, self.cV, self.cS, self.cG = Ch(nc, "cT"), Ch(nc, "cV"), Ch(nc, "cS"), Ch(nc, "cG")
        self.dchs = {}
        self._wc = {}

    def dch(self, name):
        if name not in self.dchs:
            self.dchs[name] = Ch(self.nc, "q_" + name)
        return self.dchs[name]

    def quiesce(self, eng=None):
        eng = eng or self.Y
        for ch in self.dchs.values():
            self.wait(eng, ch.ev())

    def wait(self, eng, *evs):
        key = id(eng)
        flat = []
        for ev in evs:
            if isinstance(ev, list):
                flat.extend(ev)
            else:
                flat.append(ev)
        for ev in flat:
            if ev is None:
                continue
            ch, n = ev
            if n <= 0:
                continue
            if self._wc.get((key, ch.name), 0) >= n:
                continue
            eng.wait_ge(ch.sem, n)
            self._wc[(key, ch.name)] = n

    def barrier(self):
        self.nc.all_engine_barrier()
        for eng in (self.T, self.V, self.S, self.G, self.Y):
            for ch in [self.cT, self.cV, self.cS, self.cG] + list(self.dchs.values()):
                k = (id(eng), ch.name)
                self._wc[k] = max(self._wc.get(k, 0), ch.n)


def bcol(ap, n):
    return dataclasses.replace(ap, ap=[ap.ap[0], [0, n]])


def build(taps=(), n_a_tiles=NTILES, n_b_tiles=NTILES, dbg_strict=0):
    b = Bld()
    nc = b.nc
    T, V, S, G, Y = b.T, b.V, b.S, b.G, b.Y
    cT, cV, cS, cG = b.cT, b.cV, b.cS, b.cG

    x_d = nc.dram_tensor("L0_attn_x", [B_CORE, S_LEN, D], F32, kind="ExternalInput").ap()
    wq_d = nc.dram_tensor("L0_attn_Wq", [D, D], F32, kind="ExternalInput").ap()
    wv_d = nc.dram_tensor("L0_attn_Wv", [D, D], F32, kind="ExternalInput").ap()
    wg_d = nc.dram_tensor("L0_attn_Wg", [D, D], F32, kind="ExternalInput").ap()
    wo_d = nc.dram_tensor("L0_attn_Wo", [D, D], F32, kind="ExternalInput").ap()
    beta_d = nc.dram_tensor("L0_attn_beta", [NH], F32, kind="ExternalInput").ap()
    lng_d = nc.dram_tensor("L0_attn_ln_g", [D], F32, kind="ExternalInput").ap()
    lnb_d = nc.dram_tensor("L0_attn_ln_b", [D], F32, kind="ExternalInput").ap()
    qg_d = nc.dram_tensor("L0_ffn_wq_gate", [HID, D], I32, kind="ExternalInput").ap()
    sg_d = nc.dram_tensor("L0_ffn_s_gate", [HID], F32, kind="ExternalInput").ap()
    qu_d = nc.dram_tensor("L0_ffn_wq_up", [HID, D], I32, kind="ExternalInput").ap()
    su_d = nc.dram_tensor("L0_ffn_s_up", [HID], F32, kind="ExternalInput").ap()
    qd_d = nc.dram_tensor("L0_ffn_wq_down", [D, HID], I32, kind="ExternalInput").ap()
    sd_d = nc.dram_tensor("L0_ffn_s_down", [D], F32, kind="ExternalInput").ap()
    out_d = nc.dram_tensor("out", [B_CORE, S_LEN, D], F32, kind="ExternalOutput").ap()

    xf = x_d.rearrange("b s d -> (b s) d")
    outf = out_d.rearrange("b s d -> (b s) d")
    x2_d = nc.dram_tensor("x2_scr", [B_CORE * S_LEN, D], F32, kind="Internal").ap()
    xn2_d = nc.dram_tensor("xn2_scr", [B_CORE * S_LEN, D], BF16, kind="Internal").ap()
    wgc_d = nc.dram_tensor("wg_cache", [NHB * 128, KD * 128], BF16, kind="Internal").ap()
    wac_d = nc.dram_tensor("wa_cache", [3 * KD * 128, KD * 128], BF16, kind="Internal").ap()
    woc_d = nc.dram_tensor("wo_cache", [NFS * 128, KD * 256], BF16, kind="Internal").ap()
    wuc_d = nc.dram_tensor("wu_cache", [NHB * 128, KD * 128], BF16, kind="Internal").ap()
    wdc_d = nc.dram_tensor("wd_cache", [4 * 11 * 128, 4 * 512], BF16, kind="Internal").ap()

    eye_d = nc.inline_tensor(np.eye(128).astype(ml_dtypes.bfloat16), "eye_c").ap()
    onesc_d = nc.inline_tensor(np.ones((128, 1), dtype=ml_dtypes.bfloat16), "onesc_c").ap()
    onesr_d = nc.inline_tensor(np.ones((1, 128), dtype=np.float32), "onesr_c").ap()

    taps = set(taps)
    tap_t = {}
    for name, shape, dt in [
        ("dbg_att", [128, NSUB * D], BF16), ("dbg_smal", [128, 128], F32), ("dbg_fin", [128, 64], F32),
        ("dbg_xstg", [128, 4 * D], F32), ("dbg_xnstg", [128, 2 * D], BF16),
        ("xnT", [128, KD * TT], BF16), ("statesT", [128, NH * TT], BF16),
        ("ygatedT", [128, NH * TT], BF16), ("x2", [TT, D], F32),
        ("xn2T", [128, KD * TT], BF16), ("hT", [128, NHB * TT], BF16),
    ]:
        if name in taps:
            tap_t[name] = nc.dram_tensor("tap_" + name, shape, dt, kind="ExternalOutput").ap()

    eye_sb = nc.alloc_sbuf_tensor("eye_sb", [128, 128], BF16)
    onesc_sb = nc.alloc_sbuf_tensor("onesc_sb", [128, 1], BF16)
    onesr_sb = nc.alloc_sbuf_tensor("onesr_sb", [1, 128], F32)
    beta_sb = nc.alloc_sbuf_tensor("beta_sb", [1, NH], F32)
    lamr_sb = nc.alloc_sbuf_tensor("lamr_sb", [1, NH], F32)
    lam_bc = nc.alloc_sbuf_tensor("lam_bc", [128, NH], F32)
    lng_col = nc.alloc_sbuf_tensor("lng_col", [128, KD], F32)
    lnb_col = nc.alloc_sbuf_tensor("lnb_col", [128, KD], F32)
    sg_col = nc.alloc_sbuf_tensor("sg_col", [128, NHB], F32)
    su_col = nc.alloc_sbuf_tensor("su_col", [128, NHB], F32)
    sd_col = nc.alloc_sbuf_tensor("sd_col", [128, KD], F32)
    nsg_col = nc.alloc_sbuf_tensor("nsg_col", [128, NHB], F32)
    nsu_col = nc.alloc_sbuf_tensor("nsu_col", [128, NHB], F32)
    nsd_col = nc.alloc_sbuf_tensor("nsd_col", [128, KD], F32)
    carry_sb = nc.alloc_sbuf_tensor("carry_sb", [128, NH], F32)
    smal4 = nc.alloc_sbuf_tensor("smal4", [128, 64], F32)
    eps_col = nc.alloc_sbuf_tensor("eps_col", [128, 1], F32)
    neg_col = nc.alloc_sbuf_tensor("neg_col", [128, 1], F32)
    one_col = nc.alloc_sbuf_tensor("one_col", [128, 1], F32)

    PB = [nc.alloc_psum_tensor(f"pb{i}", [128, 512], F32) for i in range(8)]
    bank_free = [None] * 8

    def col_load(dst, src1d):
        return G.dma_start(dst, src1d.rearrange("(p one) -> p one", one=1))

    def ln_group_s(stat, c0, w, ev_stats):
        c1 = c0 + w
        b.wait(S, *ev_stats)
        cS.inc(S.mul(stat[:, 16 + c0:16 + c1], stat[:, c0:c1], RD))
        ev1 = cS.inc(S.mul(stat[:, 24 + c0:24 + c1], stat[:, 8 + c0:8 + c1], RD))
        b.wait(S, ev1)
        cS.inc(S.activation(stat[:, 32 + c0:32 + c1], stat[:, 16 + c0:16 + c1], AF.Square))
        ev2 = cS.inc(S.mul(stat[:, 32 + c0:32 + c1], stat[:, 32 + c0:32 + c1], -1.0))
        b.wait(S, ev2)
        ev3 = cS.inc(S.add(stat[:, 32 + c0:32 + c1], stat[:, 32 + c0:32 + c1], eps_col[:]))
        b.wait(S, ev3)
        for k in range(w):
            ev4 = cS.inc(S.activation(stat[:, 40 + c0 + k:41 + c0 + k],
                                      stat[:, 24 + c0 + k:25 + c0 + k], AF.Ln,
                                      bias=stat[:, 32 + c0 + k:33 + c0 + k]))
        b.wait(S, ev4)
        ev5 = cS.inc(S.mul(stat[:, 40 + c0:40 + c1], stat[:, 40 + c0:40 + c1], -0.5))
        b.wait(S, ev5)
        ev6 = cS.inc(S.activation(stat[:, 48 + c0:48 + c1], stat[:, 40 + c0:40 + c1], AF.Exp))
        b.wait(S, ev6)
        for k in range(w):
            ev7 = cS.inc(S.mul(stat[:, 56 + c0 + k:57 + c0 + k],
                               stat[:, 16 + c0 + k:17 + c0 + k],
                               stat[:, 48 + c0 + k:49 + c0 + k]))
        b.wait(S, ev7)
        ev_nb = cS.inc(S.mul(stat[:, 56 + c0:56 + c1], stat[:, 56 + c0:56 + c1], -1.0))
        return ev_nb

    def ln_group(stat, c0, w, ev_stats, eng=None, ch=None):
        eng = eng or V
        ch = ch or cV
        c1 = c0 + w
        b.wait(eng, *ev_stats)
        ev_mu = ch.inc(eng.tensor_scalar(stat[:, 16 + c0:16 + c1], stat[:, c0:c1], RD, None, OP.mult))
        ev_msq = ch.inc(eng.tensor_scalar(stat[:, 24 + c0:24 + c1], stat[:, 8 + c0:8 + c1],
                                          RD, None, OP.mult))
        b.wait(eng, ev_msq)
        ev_m2 = ch.inc(eng.tensor_mul(stat[:, 32 + c0:32 + c1], stat[:, 16 + c0:16 + c1],
                                      stat[:, 16 + c0:16 + c1]))
        b.wait(eng, ev_m2)
        ev_nv = ch.inc(eng.tensor_sub(stat[:, 32 + c0:32 + c1], stat[:, 32 + c0:32 + c1],
                                      stat[:, 24 + c0:24 + c1]))
        b.wait(S, ev_nv)
        ev_lg = cS.inc(S.activation(stat[:, 40 + c0:40 + c1], stat[:, 32 + c0:32 + c1],
                                    AF.Ln, bias=eps_col[:], scale=neg_col[:]))
        b.wait(eng, ev_lg)
        ev_h = ch.inc(eng.tensor_scalar(stat[:, 32 + c0:32 + c1], stat[:, 40 + c0:40 + c1],
                                        -0.5, None, OP.mult))
        b.wait(S, ev_h)
        ev_rc = cS.inc(S.activation(stat[:, 48 + c0:48 + c1], stat[:, 32 + c0:32 + c1], AF.Exp))
        b.wait(eng, ev_rc)
        ev_t = ch.inc(eng.tensor_mul(stat[:, 56 + c0:56 + c1], stat[:, 16 + c0:16 + c1],
                                     stat[:, 48 + c0:48 + c1]))
        b.wait(eng, ev_t)
        ev_nb = ch.inc(eng.tensor_scalar(stat[:, 56 + c0:56 + c1], stat[:, 56 + c0:56 + c1],
                                         -1.0, None, OP.mult))
        return ev_nb

    with nc.Block() as _blk:
        b.dch("cc").inc(G.dma_start(eye_sb[:], eye_d[:]), 16)
        b.dch("cc").inc(G.dma_start(onesc_sb[:], onesc_d[:]), 16)
        b.dch("cc").inc(G.dma_start(onesr_sb[:], onesr_d[:]), 16)
        b.dch("cc").inc(G.dma_start(beta_sb[:], beta_d.rearrange("(one h) -> one h", one=1)), 16)
        for k in range(KD):
            b.dch("cc").inc(col_load(lng_col[:, k:k + 1], lng_d[k * 128:(k + 1) * 128]), 16)
            b.dch("cc").inc(col_load(lnb_col[:, k:k + 1], lnb_d[k * 128:(k + 1) * 128]), 16)
            b.dch("cc").inc(col_load(sd_col[:, k:k + 1], sd_d[k * 128:(k + 1) * 128]), 16)
        for k in range(NHB):
            b.dch("cc").inc(col_load(sg_col[:, k:k + 1], sg_d[k * 128:(k + 1) * 128]), 16)
            b.dch("cc").inc(col_load(su_col[:, k:k + 1], su_d[k * 128:(k + 1) * 128]), 16)
        ev_cst = b.dch("cc").ev()
        b.wait(S, ev_cst)
        ev_lam = cS.inc(S.activation(lamr_sb[:], beta_sb[:], AF.Sigmoid))
        b.wait(V, ev_cst)
        cV.inc(V.tensor_scalar(nsg_col[:], sg_col[:], -128.0, None, OP.mult))
        cV.inc(V.tensor_scalar(nsu_col[:], su_col[:], -128.0, None, OP.mult))
        cV.inc(V.tensor_scalar(nsd_col[:], sd_col[:], -128.0, None, OP.mult))
        cV.inc(V.memset(eps_col[:], EPS))
        cV.inc(V.memset(neg_col[:], -1.0))
        cV.inc(V.memset(one_col[:], 1.0))
        b.wait(T, ev_lam)
        ev_t = cT.inc(T.matmul(PB[0][:, 0:NH], onesr_sb[:], lamr_sb[:], start=True, stop=True))
        b.wait(V, ev_t)
        cV.inc(V.tensor_copy(lam_bc[:], PB[0][:, 0:NH]))
        b.quiesce(G)
        b.barrier()

        with (
            nc.sbuf_tensor("p0_int", [128, 3 * D], I32) as p0_int,
            nc.sbuf_tensor("p0_intd", [128, 3 * 2816], I32) as p0_intd,
            nc.sbuf_tensor("p0_deq", [128, 3 * D], BF16) as p0_deq,
            nc.sbuf_tensor("p0_deqd", [128, 3 * 2816], BF16) as p0_deqd,
            nc.sbuf_tensor("p0_out", [128, 3 * D], BF16) as p0_out,
            nc.sbuf_tensor("p0_outd", [128, 3 * 2816], BF16) as p0_outd,
        ):
            sfree = {}
            grot = 0
            _sc0 = nc.enter_named_scope("p0gu", False)
            for wi, (src, scol, ncol, dst) in enumerate(
                    [(qg_d, sg_col, nsg_col, wgc_d), (qu_d, su_col, nsu_col, wuc_d)]):
                for hb in range(NHB):
                    i = wi * NHB + hb
                    sl = i % 3
                    b.wait(G, sfree.get(("int", sl)))
                    ev_ld = b.dch(f"p0i{sl}").inc(G.dma_start(p0_int[:, sl * D:(sl + 1) * D],
                                               src[hb * 128:(hb + 1) * 128, :]), 16)
                    b.wait(S, ev_ld, sfree.get(("deq", sl)))
                    ev_deq = cS.inc(S.activation(p0_deq[:, sl * D:(sl + 1) * D],
                                                 p0_int[:, sl * D:(sl + 1) * D], AF.Identity,
                                                 bias=ncol[:, hb:hb + 1], scale=scol[:, hb:hb + 1]))
                    sfree[("int", sl)] = ev_deq
                    ev_e = None
                    for g in range(4):
                        rb = grot % 6
                        grot += 1
                        b.wait(T, ev_deq, bank_free[rb])
                        for kk in range(4):
                            kd = g * 4 + kk
                            ins = T.matmul(PB[rb][:, kk * 128:(kk + 1) * 128],
                                           p0_deq[:, sl * D + kd * 128: sl * D + (kd + 1) * 128],
                                           eye_sb[:], start=True, stop=True)
                        ev_tr = cT.inc(ins)
                        eng, ch = (V, cV) if g % 2 == 0 else (S, cS)
                        b.wait(eng, ev_tr, sfree.get(("out", sl)))
                        if eng is V:
                            ev_e = ch.inc(V.tensor_copy(
                                p0_out[:, sl * D + g * 512: sl * D + (g + 1) * 512], PB[rb][:, 0:512]))
                        else:
                            ev_e = ch.inc(S.activation(
                                p0_out[:, sl * D + g * 512: sl * D + (g + 1) * 512], PB[rb][:, 0:512],
                                AF.Copy))
                        bank_free[rb] = ev_e
                        if g == 3:
                            ev_e4 = [bank_free[(grot - 4 + k) % 6] for k in range(4)]
                    sfree[("deq", sl)] = ev_e
                    b.wait(Y, *ev_e4)
                    sfree[("out", sl)] = b.dch(f"p0o{sl}").inc(
                        Y.dma_start(dst[hb * 128:(hb + 1) * 128, :], p0_out[:, sl * D:(sl + 1) * D]), 16)
            nc.leave_named_scope("p0gu", _sc0[0], False)
            _sc1 = nc.enter_named_scope("p0d", False)
            for ec in range(KD):
                es, ee = ec // 4, ec % 4
                for half in range(2):
                    i = ec * 2 + half
                    sl = i % 3
                    h0 = half * 2816
                    b.wait(G, sfree.get(("intd", sl)))
                    ev_ld = b.dch(f"p0id{sl}").inc(G.dma_start(p0_intd[:, sl * 2816:(sl + 1) * 2816],
                                               qd_d[ec * 128:(ec + 1) * 128, h0:h0 + 2816]), 16)
                    b.wait(S, ev_ld, sfree.get(("deqd", sl)))
                    ev_deq = cS.inc(S.activation(p0_deqd[:, sl * 2816:(sl + 1) * 2816],
                                                 p0_intd[:, sl * 2816:(sl + 1) * 2816], AF.Identity,
                                                 bias=nsd_col[:, ec:ec + 1], scale=sd_col[:, ec:ec + 1]))
                    sfree[("intd", sl)] = ev_deq
                    ev_el = []
                    for g in range(6):
                        npc = min(4, 22 - g * 4)
                        if npc <= 0:
                            break
                        rb = grot % 6
                        grot += 1
                        b.wait(T, ev_deq, bank_free[rb])
                        for kk in range(npc):
                            kh = g * 4 + kk
                            ins = T.matmul(PB[rb][:, kk * 128:(kk + 1) * 128],
                                           p0_deqd[:, sl * 2816 + kh * 128: sl * 2816 + (kh + 1) * 128],
                                           eye_sb[:], start=True, stop=True)
                        ev_tr = cT.inc(ins)
                        eng, ch = (V, cV) if g % 2 == 0 else (S, cS)
                        b.wait(eng, ev_tr, sfree.get(("outd", sl)))
                        if eng is V:
                            ev_e = ch.inc(V.tensor_copy(
                                p0_outd[:, sl * 2816 + g * 512: sl * 2816 + g * 512 + npc * 128],
                                PB[rb][:, 0:npc * 128]))
                        else:
                            ev_e = ch.inc(S.activation(
                                p0_outd[:, sl * 2816 + g * 512: sl * 2816 + g * 512 + npc * 128],
                                PB[rb][:, 0:npc * 128], AF.Copy))
                        bank_free[rb] = ev_e
                        ev_el.append(ev_e)
                    sfree[("deqd", sl)] = ev_el[-1]
                    b.wait(Y, *ev_el)
                    last = None
                    lo = 0
                    while lo < 22:
                        khg = half * 22 + lo
                        khc, k4 = khg // 4, khg % 4
                        nk = min(4 - k4, 22 - lo)
                        row0 = (es * 11 + khc) * 128
                        dstv = wdc_d[row0:row0 + 128, :].rearrange(
                            "p (k e) -> p k e", e=128)[:, 4 * k4 + ee: 4 * (k4 + nk - 1) + ee + 1:4, :]
                        srcv = p0_outd[:, sl * 2816 + lo * 128: sl * 2816 + (lo + nk) * 128] \
                            .rearrange("p (k e) -> p k e", e=128)
                        last = b.dch(f"p0od{sl}").inc(Y.dma_start(dstv, srcv), 16)
                        lo += nk
                    sfree[("outd", sl)] = last
            nc.leave_named_scope("p0d", _sc1[0], False)
            b.quiesce()
            b.quiesce(G)
            b.barrier()
            bank_free = [None] * 8

        with (
            nc.sbuf_tensor("big0", [128, KD * TT], BF16) as big0,
            nc.sbuf_tensor("big1", [128, NH * TT], BF16) as big1,
            nc.sbuf_tensor("xstg", [128, 4 * D], F32) as xstg,
            nc.sbuf_tensor("xnstg", [128, 2 * D], BF16) as xnstg,
            nc.sbuf_tensor("trash", [128, D], BF16) as trash,
            nc.sbuf_tensor("wf32", [128, 3 * D], F32) as wf32,
            nc.sbuf_tensor("wbf", [128, 2 * D], BF16) as wbf,
            nc.sbuf_tensor("wbf3", [128, 3 * D], BF16) as wbf3,
            nc.sbuf_tensor("wo_bf", [128, 2 * KD * 256], BF16) as wo_bf,
            nc.sbuf_tensor("ysq", [128, 4 * 512], BF16) as ysq,
            nc.sbuf_tensor("gs_stg", [128, 10 * 512], BF16) as gs_stg,
            nc.sbuf_tensor("bc_RQ", [128, 2 * TT], F32) as bc_RQ,
            nc.sbuf_tensor("rows", [1, 3 * TT], F32) as rows,
            nc.sbuf_tensor("smal", [128, 144], F32) as smal,
            nc.sbuf_tensor("dbg_fin_sb", [128, 64], F32) as dbg_fin_sb,
        ):
            print("SBUF remaining inside A scope:", nc.sbuf_bytes_remaining)
            a_slot_free = [None] * 4
            a_xn_free = [None] * 2
            a_sum_ev = {}
            a_copy_ev = {}
            a_ev_cast = {}
            a_ev_cw = {}
            a_ev_mm_done = {}
            a_wo_cast = {}
            a_fs_done = {}
            a_wocw = {}
            a_sq_instr = {}
            gs_free = [None] * 10
            a_rot = [0]
            keep_barriers = bool(taps) or dbg_strict >= 1
            for t in range(n_a_tiles):
                _scA = nc.enter_named_scope(f"A{t}", False)
                bb, half = t // 2, t % 2
                r0 = bb * S_LEN + half * TT
                ev_xstg_free = a_slot_free
                ev_xn_free = a_xn_free
                ev_apply = [None] * NSUB
                ev_sums = {}
                for grp in range(2):
                    for s in range(grp * 4, grp * 4 + 4):
                        sl = s % 4
                        b.wait(G, ev_xstg_free[sl])
                        ev_x = b.dch(f"ax{sl}").inc(G.dma_start(
                            xstg[:, sl * D:(sl + 1) * D],
                            xf[r0 + s * 128: r0 + (s + 1) * 128, :]), 16)
                        xs = xstg[:, sl * D:(sl + 1) * D]
                        b.wait(V, ev_x, a_copy_ev.get(s))
                        ev_sums[2 * s] = cV.inc(V.reduce_sum(smal[:, s:s + 1], xs, axis=AX))
                        a_sum_ev[s] = ev_sums[2 * s]
                        b.wait(S, ev_x)
                        ev_sums[2 * s + 1] = cS.inc(S.activation(trash[:], xs, AF.Square,
                                                                 accum_out=smal[:, 8 + s:9 + s]))
                    ev_nb = ln_group(smal, grp * 4, 4, [ev_sums[k] for k in range(grp * 8, grp * 8 + 8)])
                    for s in range(grp * 4, grp * 4 + 4):
                        sl = s % 4
                        c0 = s % 2
                        xs = xstg[:, sl * D:(sl + 1) * D]
                        b.wait(S, ev_nb, ev_xn_free[c0])
                        ev_xn = cS.inc(S.activation(xnstg[:, c0 * D:(c0 + 1) * D], xs, AF.Identity,
                                                    scale=smal[:, 48 + s:49 + s],
                                                    bias=smal[:, 56 + s:57 + s]))
                        ev_xstg_free[sl] = ev_xn
                        ev_apply[s] = ev_xn
                        for g in range(4):
                            rb = 4 * (s % 2) + g
                            b.wait(T, ev_xn, bank_free[rb])
                            for kk in range(4):
                                kd = g * 4 + kk
                                ins = T.matmul(PB[rb][:, kk * 128:(kk + 1) * 128],
                                               xnstg[:, c0 * D + kd * 128: c0 * D + (kd + 1) * 128],
                                               eye_sb[:], start=True, stop=True)
                            ev_tr = cT.inc(ins)
                            b.wait(V, ev_tr)
                            dst = big0[:].rearrange("p (kd t) -> p kd t", kd=KD)[
                                :, g * 4:(g + 1) * 4, s * 128:(s + 1) * 128]
                            src = PB[rb][:, 0:512].rearrange("p (kd e) -> p kd e", kd=4)
                            ev_e = cV.inc(V.tensor_copy(dst, src))
                            bank_free[rb] = ev_e
                            if g == 3:
                                ev_xn_free[c0] = ev_e
                ev_xnT = cV.ev()
                if "xnT" in tap_t and t == 0:
                    b.wait(Y, ev_xnT)
                    b.dch("tp").inc(Y.dma_start(tap_t["xnT"][:], big0[:]), 16)
                if "dbg_xstg" in tap_t and t == 0:
                    b.wait(Y, ev_xnT)
                    b.dch("tp").inc(Y.dma_start(tap_t["dbg_xstg"][:], xstg[:]), 16)
                    b.dch("tp").inc(Y.dma_start(tap_t["dbg_xnstg"][:], xnstg[:]), 16)

                ev_cast = a_ev_cast
                ev_cw = a_ev_cw
                ev_mm_done = a_ev_mm_done
                rot = [2, 3, 4]
                ev_si_hb = {}
                ev_gate_done = [None] * NH
                ev_scan_done = [None] * NH
                ev_yT_last = None
                ev_R = ev_Q = None
                ev_sq_instr = a_sq_instr

                def emit_sy():
                    for tb in range(NTB):
                        b.wait(T, ev_yT_last, bank_free[5])
                        for hb in range(NH):
                            ins = T.matmul(PB[5][0:1, 0:512], onesc_sb[:],
                                           big1[:, hb * TT + tb * 512: hb * TT + (tb + 1) * 512],
                                           start=(hb == 0), stop=(hb == NH - 1))
                        ev_sy = cT.inc(ins)
                        b.wait(V, ev_sy)
                        ev_r = cV.inc(V.tensor_copy(rows[:, tb * 512:(tb + 1) * 512], PB[5][0:1, 0:512]))
                        bank_free[5] = ev_r

                def emit_ssq():
                    for tb in range(NTB):
                        b.wait(T, bank_free[6])
                        for hb in range(NH):
                            ysl = (tb * NH + hb) % 4
                            yv = ysq[:, ysl * 512:(ysl + 1) * 512]
                            y_sl = big1[:, hb * TT + tb * 512: hb * TT + (tb + 1) * 512]
                            gsq = t * 2 * NH + tb * NH + hb
                            b.wait(V, ev_sq_instr.get(gsq - 4))
                            ev_q2 = cV.inc(V.tensor_tensor(yv, y_sl, y_sl, OP.mult))
                            b.wait(T, ev_q2)
                            ins = T.matmul(PB[6][0:1, 0:512], onesc_sb[:], yv,
                                           start=(hb == 0), stop=(hb == NH - 1))
                            ev_sq_instr[gsq] = cT.inc(ins)
                        b.wait(V, ev_sq_instr[t * 2 * NH + tb * NH + NH - 1])
                        ev_r = cV.inc(V.tensor_copy(rows[:, TT + tb * 512: TT + (tb + 1) * 512],
                                                    PB[6][0:1, 0:512]))
                        bank_free[6] = ev_r

                def emit_rows():
                    nonlocal ev_R, ev_Q
                    ev1 = cV.inc(V.tensor_scalar(rows[:, 2 * TT:3 * TT], rows[:, 0:TT], RD, None, OP.mult))
                    ev2 = cV.inc(V.tensor_scalar(rows[:, 0:TT], rows[:, TT:2 * TT], RD, None, OP.mult))
                    b.wait(V, ev2)
                    ev3 = cV.inc(V.tensor_mul(rows[:, TT:2 * TT], rows[:, 2 * TT:3 * TT],
                                              rows[:, 2 * TT:3 * TT]))
                    b.wait(V, ev3)
                    ev4 = cV.inc(V.tensor_sub(rows[:, TT:2 * TT], rows[:, TT:2 * TT], rows[:, 0:TT]))
                    b.wait(S, ev4)
                    ev5 = cS.inc(S.activation(rows[:, 0:TT], rows[:, TT:2 * TT], AF.Ln,
                                              bias=eps_col[0:1, :], scale=neg_col[0:1, :]))
                    b.wait(V, ev5)
                    evh = cV.inc(V.tensor_scalar(rows[:, 0:TT], rows[:, 0:TT], -0.5, None, OP.mult))
                    b.wait(S, evh)
                    ev_R = cS.inc(S.activation(rows[:, TT:2 * TT], rows[:, 0:TT], AF.Exp))
                    b.wait(V, ev_R)
                    ev_Q = cV.inc(V.tensor_mul(rows[:, 2 * TT:3 * TT], rows[:, 2 * TT:3 * TT],
                                               rows[:, TT:2 * TT]))

                def emit_bcast():
                    for tb in range(NTB):
                        b.wait(T, ev_R, bank_free[5])
                        ev_b1 = cT.inc(T.matmul(PB[5][:, 0:512], onesr_sb[:],
                                                rows[0:1, TT + tb * 512: TT + (tb + 1) * 512],
                                                start=True, stop=True))
                        b.wait(V, ev_b1)
                        ev_e = cV.inc(V.tensor_copy(bc_RQ[:, tb * 512:(tb + 1) * 512], PB[5][:, 0:512]))
                        bank_free[5] = ev_e
                        b.wait(T, ev_Q, bank_free[6])
                        ev_b2 = cT.inc(T.matmul(PB[6][:, 0:512], onesr_sb[:],
                                                rows[0:1, 2 * TT + tb * 512: 2 * TT + (tb + 1) * 512],
                                                start=True, stop=True))
                        b.wait(V, ev_b2)
                        ev_e = cV.inc(V.tensor_copy(bc_RQ[:, TT + tb * 512: TT + (tb + 1) * 512],
                                                    PB[6][:, 0:512]))
                        bank_free[6] = ev_e

                def emit_apply():
                    for hb in range(NH):
                        y_sl = big1[:, hb * TT:(hb + 1) * TT]
                        t1 = ysq[:, 0:1024]
                        t2 = ysq[:, 1024:2048]
                        cV.inc(V.scalar_tensor_tensor(t1, y_sl, lng_col[:, hb:hb + 1],
                                                      bc_RQ[:, 0:TT], OP.mult, OP.mult))
                        ev_o2 = cV.inc(V.scalar_tensor_tensor(t2, bc_RQ[:, TT:2 * TT],
                                                              lng_col[:, hb:hb + 1],
                                                              bcol(lnb_col[:, hb:hb + 1], TT),
                                                              OP.mult, OP.subtract))
                        b.wait(V, ev_o2)
                        cV.inc(V.tensor_tensor(y_sl, t1, t2, OP.subtract))

                def emit_gate(k):
                    for tb in range(NTB):
                        gsl = (2 * k + tb) % 10
                        st_sl = big1[:, k * TT + tb * 512: k * TT + (tb + 1) * 512]
                        gv = gs_stg[:, gsl * 512:(gsl + 1) * 512]
                        b.wait(V, ev_si_hb[(k, tb)])
                        ev_g = cV.inc(V.tensor_tensor(st_sl, gv, st_sl, OP.mult))
                        gs_free[gsl] = ev_g
                        if tb == NTB - 1:
                            ev_gate_done[k] = ev_g

                blk_i = t * 3 * NH
                for wname, wsrc in (("v", wv_d), ("q", wq_d), ("g", wg_d)):
                    for hb in range(NH):
                        if wname == "g" and hb >= 5:
                            emit_gate(hb - 5)
                        i = blk_i
                        wi3 = {"v": 1, "q": 0, "g": 2}[wname]
                        crow = (wi3 * KD + hb) * 128
                        if t == 0:
                            sl2 = i % 3
                            sl2b = i % 2
                            fv = wf32[:, sl2 * D:(sl2 + 1) * D]
                            bv = wbf[:, sl2b * D:(sl2b + 1) * D]
                            b.wait(Y, ev_cast.get(i - 3))
                            for k4 in range(4):
                                srcv = wsrc[k4 * 512:(k4 + 1) * 512, hb * 128:(hb + 1) * 128] \
                                    .rearrange("(kd p) e -> p kd e", p=128)
                                dstv = fv.rearrange("p (kd e) -> p kd e", kd=KD)[:, k4 * 4:(k4 + 1) * 4, :]
                                ev_ld = b.dch(f"aw{sl2}").inc(Y.dma_start(dstv, srcv), 16)
                            eng, ch = (S, cS) if i % 2 == 0 else (V, cV)
                            b.wait(eng, ev_ld, ev_mm_done.get(i - 2), ev_cw.get(i - 2))
                            if eng is S:
                                ev_cast[i] = ch.inc(S.activation(bv, fv, AF.Copy))
                            else:
                                ev_cast[i] = ch.inc(V.tensor_copy(bv, fv))
                            b.wait(Y, ev_cast[i])
                            ev_cw[i] = b.dch(f"acw{sl2b}").inc(
                                Y.dma_start(wac_d[crow:crow + 128, :], bv), 16)
                        else:
                            sl2b = i % 3
                            bv = wbf3[:, sl2b * D:(sl2b + 1) * D]
                            b.wait(Y, ev_mm_done.get(i - 3))
                            ev_cast[i] = b.dch(f"ab{sl2b}").inc(
                                Y.dma_start(bv, wac_d[crow:crow + 128, :]), 16)
                        for tb in range(NTB):
                            rb = rot[a_rot[0] % 3]
                            a_rot[0] += 1
                            b.wait(T, ev_cast[i], ev_xnT, bank_free[rb])
                            for kd in range(KD):
                                ins = T.matmul(PB[rb][:, 0:512], bv[:, kd * 128:(kd + 1) * 128],
                                               big0[:, kd * TT + tb * 512: kd * TT + (tb + 1) * 512],
                                               start=(kd == 0), stop=(kd == KD - 1))
                            ev_mm = cT.inc(ins)
                            st_sl = big1[:, hb * TT + tb * 512: hb * TT + (tb + 1) * 512]
                            if wname == "v":
                                if tb == 0 and half == 0:
                                    init = 0.0
                                elif tb == 0:
                                    init = carry_sb[:, hb:hb + 1]
                                else:
                                    init = big1[:, hb * TT + 511: hb * TT + 512]
                                    b.wait(V, ev_sc0)
                                b.wait(V, ev_mm)
                                ev_sc = cV.inc(V.tensor_tensor_scan(
                                    st_sl, bcol(lam_bc[:, hb:hb + 1], 512), PB[rb][:, 0:512],
                                    init, OP.mult, OP.add))
                                ev_sc0 = ev_sc
                                bank_free[rb] = ev_sc
                                if tb == NTB - 1:
                                    ev_scan_done[hb] = ev_sc
                                    if half == 0:
                                        b.wait(V, ev_sc)
                                        cV.inc(V.tensor_copy(carry_sb[:, hb:hb + 1],
                                                             big1[:, (hb + 1) * TT - 1:(hb + 1) * TT]))
                            elif wname == "q":
                                b.wait(V, ev_mm)
                                ev_y = cV.inc(V.tensor_tensor(st_sl, PB[rb][:, 0:512], st_sl, OP.mult))
                                bank_free[rb] = ev_y
                                ev_yT_last = ev_y
                            else:
                                gsl = (2 * hb + tb) % 10
                                gv = gs_stg[:, gsl * 512:(gsl + 1) * 512]
                                b.wait(S, ev_mm, gs_free[gsl])
                                ev_si = cS.inc(S.activation(gv, PB[rb][:, 0:512], AF.Silu))
                                bank_free[rb] = ev_si
                                ev_si_hb[(hb, tb)] = ev_si
                        ev_mm_done[i] = cT.ev()
                        blk_i += 1
                        if wname == "g":
                            if hb == 0:
                                emit_sy()
                            elif hb == 1:
                                emit_ssq()
                            elif hb == 2:
                                emit_rows()
                            elif hb == 3:
                                emit_bcast()
                            elif hb == 4:
                                emit_apply()

                    if wname == "v" and "statesT" in tap_t and t == 0:
                        b.wait(Y, *ev_scan_done)
                        b.dch("tp").inc(Y.dma_start(tap_t["statesT"][:], big1[:]), 16)

                for k in range(NH - 5, NH):
                    emit_gate(k)

                if "ygatedT" in tap_t and t == 0:
                    b.wait(Y, *ev_gate_done)
                    b.dch("tp").inc(Y.dma_start(tap_t["ygatedT"][:], big1[:]), 16)
                ev_wo_cast = a_wo_cast
                ev_fs_done = a_fs_done
                ev_wocw_d = a_wocw
                ev_sum_done = {}
                ev_last_casts = [ev_cast.get(3 * NH - 3), ev_cast.get(3 * NH - 2), ev_cast.get(3 * NH - 1)]
                ev_wocw = ev_wocw_d
                for fsb in range(NFS):
                    gf = t * NFS + fsb
                    osl = fsb % 2
                    if t == 0:
                        for ke in range(KD):
                            i = fsb * KD + ke
                            islot = i % 8
                            fv = wf32[:, islot * 256:(islot + 1) * 256]
                            b.wait(Y, ev_wo_cast.get(i - 8), *(ev_last_casts if i < 8 else []))
                            ev_ld = b.dch(f"ao{islot}").inc(Y.dma_start(
                                fv, wo_d[ke * 128:(ke + 1) * 128, fsb * 256:(fsb + 1) * 256]), 16)
                            eng, ch = (S, cS) if i % 2 == 0 else (V, cV)
                            wdst = wo_bf[:, osl * KD * 256 + ke * 256: osl * KD * 256 + (ke + 1) * 256]
                            b.wait(eng, ev_ld, ev_fs_done.get(gf - 2), ev_wocw.get(gf - 2))
                            if eng is S:
                                ev_wo_cast[i] = ch.inc(S.activation(wdst, fv, AF.Copy))
                            else:
                                ev_wo_cast[i] = ch.inc(V.tensor_copy(wdst, fv))
                        b.wait(Y, *[ev_wo_cast[fsb * KD + k] for k in range(KD)])
                        ev_wocw[gf] = b.dch(f"wocw{osl}").inc(Y.dma_start(
                            woc_d[fsb * 128:(fsb + 1) * 128, :],
                            wo_bf[:, osl * KD * 256:(osl + 1) * KD * 256]), 16)
                    else:
                        b.wait(Y, ev_fs_done.get(gf - 2), ev_wocw.get(gf - 2))
                        ev_l = b.dch(f"wocw{osl}").inc(Y.dma_start(
                            wo_bf[:, osl * KD * 256:(osl + 1) * KD * 256],
                            woc_d[fsb * 128:(fsb + 1) * 128, :]), 16)
                        ev_wocw[gf] = ev_l
                        for ke in range(KD):
                            ev_wo_cast[fsb * KD + ke] = ev_l
                    for s in range(NSUB):
                        rb = (fsb * NSUB + s) % 4
                        b.wait(T, bank_free[rb])
                        for ke in range(KD):
                            b.wait(T, ev_wo_cast[fsb * KD + ke], ev_gate_done[ke])
                            ins = T.matmul(PB[rb][:, 0:256],
                                           big1[:, ke * TT + s * 128: ke * TT + (s + 1) * 128],
                                           wo_bf[:, osl * KD * 256 + ke * 256: osl * KD * 256 + (ke + 1) * 256],
                                           start=(ke == 0), stop=(ke == KD - 1))
                        ev_mm = cT.inc(ins)
                        dsl = big0[:, s * 2048 + fsb * 256: s * 2048 + (fsb + 1) * 256]
                        b.wait(V, ev_mm)
                        ev_rs = cV.inc(V.reduce_sum(smal[:, 64 + s * 8 + fsb: 65 + s * 8 + fsb],
                                                    PB[rb][:, 0:256], axis=AX))
                        b.wait(S, ev_rs)
                        ev_ev = cS.inc(S.activation(dsl, PB[rb][:, 0:256], AF.Copy))
                        bank_free[rb] = [ev_ev, ev_rs]
                        ev_sum_done[(s, fsb)] = [ev_ev, ev_rs]
                    ev_fs_done[gf] = cT.ev()
                if "dbg_att" in tap_t and t == 0:
                    b.wait(Y, *[ev_sum_done[(s_, f_)] for s_ in range(NSUB) for f_ in range(NFS)])
                    b.dch("tp").inc(Y.dma_start(tap_t["dbg_att"][:], big0[:]), 16)
                ev_x2_tap = []
                ev_stage_free = a_slot_free
                for s in range(NSUB):
                    for fsb in range(NFS):
                        b.wait(V, *ev_sum_done[(s, fsb)])
                    dsl_full = big0[:, s * 2048:(s + 1) * 2048]
                    cV.inc(V.scalar_tensor_tensor(trash[:], dsl_full, 1.0, dsl_full,
                                                  OP.mult, OP.mult, accum_out=smal[:, 128:129]))
                    ev_red = cV.inc(V.tensor_reduce(smal[:, 129:130], smal[:, 64 + s * 8:64 + s * 8 + 8],
                                                    AX, OP.add))
                    b.wait(V, ev_red)
                    ev_mu6 = cV.inc(V.tensor_scalar(smal[:, 130:131], smal[:, 129:130], RD, None, OP.mult))
                    ev_msq6 = cV.inc(V.tensor_scalar(smal[:, 131:132], smal[:, 128:129], RD, None, OP.mult))
                    b.wait(V, ev_msq6)
                    ev_nv = cV.inc(V.scalar_tensor_tensor(smal[:, 132:133], smal[:, 130:131],
                                                          smal[:, 130:131], smal[:, 131:132],
                                                          OP.mult, OP.subtract))
                    b.wait(S, ev_nv)
                    ev_sg = cS.inc(S.activation(smal[:, 133:134], smal[:, 132:133], AF.Ln,
                                                bias=eps_col[:], scale=neg_col[:]))
                    b.wait(V, ev_sg)
                    evh6 = cV.inc(V.tensor_scalar(smal[:, 133:134], smal[:, 133:134],
                                                  -0.5, None, OP.mult))
                    b.wait(S, evh6)
                    ev_rc6 = cS.inc(S.activation(smal[:, 134:135], smal[:, 133:134], AF.Exp))
                    b.wait(V, ev_rc6)
                    ev_nb = cV.inc(V.tensor_scalar(smal[:, 135:136], smal[:, 130:131], smal[:, 134:135],
                                                   -1.0, OP.mult, OP.mult))
                    if "dbg_fin" in tap_t and t == 0:
                        b.wait(V, ev_nb)
                        cV.inc(V.tensor_copy(dbg_fin_sb[:, s * 8:(s + 1) * 8], smal[:, 128:136]))
                    ssl = s % 2
                    slv = xstg[:, (2 + ssl) * D:(3 + ssl) * D]
                    b.wait(G, ev_stage_free[2 + ssl])
                    ev_xr = b.dch(f"ax{2 + ssl}").inc(G.dma_start(
                        slv, xf[r0 + s * 128: r0 + (s + 1) * 128, :]), 16)
                    b.wait(V, ev_nb, ev_xr)
                    ev_p = cV.inc(V.tensor_scalar(slv, slv, smal[:, 135:136], None, OP.add))
                    b.wait(V, ev_p)
                    ev_o = cV.inc(V.scalar_tensor_tensor(slv, dsl_full, smal[:, 134:135], slv,
                                                         OP.mult, OP.add))
                    b.wait(G, ev_o)
                    ev_w = b.dch(f"ax{2 + ssl}").inc(G.dma_start(
                        x2_d[r0 + s * 128: r0 + (s + 1) * 128, :], slv), 16)
                    if "x2" in tap_t and t == 0:
                        ev_w = b.dch(f"ax{2 + ssl}").inc(G.dma_start(
                            tap_t["x2"][s * 128:(s + 1) * 128, :], slv), 16)
                    b.wait(S, a_sum_ev[s])
                    ev_su = cS.inc(S.mul(smal4[:, s:s + 1], smal[:, s:s + 1], 1.0))
                    a_copy_ev[s] = ev_su
                    ev_sq = cV.inc(V.scalar_tensor_tensor(trash[:], slv, 1.0, slv,
                                                          OP.mult, OP.mult,
                                                          accum_out=smal4[:, 8 + s:9 + s]))
                    ev_nb4 = ln_group_s(smal4, s, 1, [ev_su, ev_sq])
                    a6sl = s % 2
                    b.wait(S, ev_nb4, a_xn_free[a6sl])
                    ev_ap4 = cS.inc(S.activation(xnstg[:, a6sl * D:(a6sl + 1) * D], slv,
                                                 AF.Identity, scale=smal4[:, 48 + s:49 + s],
                                                 bias=smal4[:, 56 + s:57 + s]))
                    b.wait(Y, ev_ap4)
                    ev_wx = b.dch(f"xnw{a6sl}").inc(Y.dma_start(
                        xn2_d[r0 + s * 128: r0 + (s + 1) * 128, :],
                        xnstg[:, a6sl * D:(a6sl + 1) * D]), 16)
                    a_xn_free[a6sl] = ev_wx
                    ev_stage_free[2 + ssl] = [ev_w, ev_ap4]
                if "dbg_fin" in tap_t and t == 0:
                    b.wait(Y, cV.ev())
                    b.dch("tp").inc(Y.dma_start(tap_t["dbg_fin"][:], dbg_fin_sb[:]), 16)
                if "dbg_smal" in tap_t and t == 0:
                    b.wait(Y, cV.ev(), cS.ev())
                    b.dch("tp").inc(Y.dma_start(tap_t["dbg_smal"][:], smal[:]), 16)
                nc.leave_named_scope(f"A{t}", _scA[0], False)
                if keep_barriers or t == n_a_tiles - 1:
                    b.quiesce()
                    b.quiesce(G)
                    b.barrier()
                    bank_free = [None] * 8

        with (
            nc.sbuf_tensor("bigh", [128, NHB * TT], BF16) as bigh,
            nc.sbuf_tensor("big2", [128, KD * TT], BF16) as big2,
            nc.sbuf_tensor("bxstg", [128, 4 * D], F32) as bxstg,
            nc.sbuf_tensor("bxnstg", [128, 4 * D], BF16) as bxnstg,
            nc.sbuf_tensor("btrash", [128, D], BF16) as btrash,
            nc.sbuf_tensor("wgu", [128, 4 * D], BF16) as wgu,
            nc.sbuf_tensor("wdr", [128, 2 * 2048], BF16) as wdr,
            nc.sbuf_tensor("bgs", [128, 2 * 512], BF16) as bgs,
            nc.sbuf_tensor("bstat", [128, 112], F32) as bstat,
        ):
            print("SBUF remaining inside B scope:", nc.sbuf_bytes_remaining)
            b_slot_free = [None] * 4
            b_xn_free = [None] * 4
            b_last_stt = [None]
            b_wld = {}
            b_wfree = {}
            b_chunk = {}
            b_chunk_free = {}
            bgs_free = [None, None]
            keep_barriers = bool(taps) or dbg_strict >= 1
            b_apply = {}
            b_xn2T_tb = {}
            b_pref = {}

            def b1_ld(tt, s):
                sl4 = s % 4
                ev = b_pref.pop((tt, s), None)
                if ev is None:
                    b.wait(S, b_xn_free[sl4])
                    ev = b.dch(f"bn{sl4}").inc(S.dma_start(
                        bxnstg[:, sl4 * D:(sl4 + 1) * D],
                        xn2_d[tt * TT + s * 128: tt * TT + (s + 1) * 128, :]), 16)
                b_apply[(tt, s)] = ev

            def b1_back(tt, ss):
                if 0 in ss:
                    b.wait(S, b_last_stt[0])
                for s in ss:
                    c0 = s % 4
                    eng, ch = (S, cS) if s < 4 else (V, cV)
                    for g in range(4):
                        rb = 4 * (s % 2) + g
                        b.wait(T, b_apply[(tt, s)], bank_free[rb])
                        for kk in range(4):
                            kd = g * 4 + kk
                            ins = T.matmul(PB[rb][:, kk * 128:(kk + 1) * 128],
                                           bxnstg[:, c0 * D + kd * 128: c0 * D + (kd + 1) * 128],
                                           eye_sb[:], start=True, stop=True)
                        ev_tr = cT.inc(ins)
                        b.wait(eng, ev_tr)
                        dst = big2[:].rearrange("p (kd t) -> p kd t", kd=KD)[
                            :, g * 4:(g + 1) * 4, s * 128:(s + 1) * 128]
                        src = PB[rb][:, 0:512].rearrange("p (kd e) -> p kd e", kd=4)
                        if eng is S:
                            ev_e = ch.inc(S.activation(dst, src, AF.Copy))
                        else:
                            ev_e = ch.inc(V.tensor_copy(dst, src))
                        bank_free[rb] = ev_e
                        if g == 3:
                            b_xn_free[c0] = ev_e
                    if s == 3:
                        b_xn2T_tb[(tt, 0)] = [cS.ev()]
                    if s == 7:
                        b_xn2T_tb[(tt, 1)] = [cS.ev(), cV.ev()]

            for t in range(n_b_tiles):
                _scB = nc.enter_named_scope(f"B{t}", False)
                r0 = t * TT
                for s in range(4):
                    b1_ld(t, s)
                for s in range(NSUB):
                    b1_back(t, (s,))
                    if s + 4 < NSUB:
                        b1_ld(t, s + 4)
                if "xn2T" in tap_t and t == 0:
                    b.wait(Y, *b_xn2T_tb[(0, 1)], *b_xn2T_tb[(0, 0)])
                    b.dch("tp").inc(Y.dma_start(tap_t["xn2T"][:], big2[:]), 16)

                ev_wld = b_wld
                ev_wfree = b_wfree
                for hb in range(NHB):
                    ghb = t * NHB + hb
                    sl = hb % 2
                    for wi, cache in enumerate((wgc_d, wuc_d)):
                        b.wait(Y, ev_wfree.get((wi, ghb - 2)))
                        ev_wld[(wi, ghb)] = b.dch(f"bw{wi * 2 + sl}").inc(Y.dma_start(
                            wgu[:, (wi * 2 + sl) * D:(wi * 2 + sl + 1) * D],
                            cache[hb * 128:(hb + 1) * 128, :]), 16)
                    for tb in range(NTB):
                        i = 2 * hb + tb
                        rbg, rbu = (2 * i) % 8, (2 * i + 1) % 8
                        b.wait(T, ev_wld[(0, ghb)], *b_xn2T_tb[(t, tb)])
                        b.wait(T, bank_free[rbg])
                        for kd in range(KD):
                            ins = T.matmul(PB[rbg][:, 0:512],
                                           wgu[:, sl * D + kd * 128: sl * D + (kd + 1) * 128],
                                           big2[:, kd * TT + tb * 512: kd * TT + (tb + 1) * 512],
                                           start=(kd == 0), stop=(kd == KD - 1))
                        ev_gmm = cT.inc(ins)
                        b.wait(T, ev_wld[(1, ghb)], bank_free[rbu])
                        for kd in range(KD):
                            ins = T.matmul(PB[rbu][:, 0:512],
                                           wgu[:, (2 + sl) * D + kd * 128: (2 + sl) * D + (kd + 1) * 128],
                                           big2[:, kd * TT + tb * 512: kd * TT + (tb + 1) * 512],
                                           start=(kd == 0), stop=(kd == KD - 1))
                        ev_umm = cT.inc(ins)
                        if tb == NTB - 1:
                            ev_wfree[(0, ghb)] = ev_gmm
                            ev_wfree[(1, ghb)] = ev_umm
                        gv = bgs[:, tb * 512:(tb + 1) * 512]
                        b.wait(S, ev_gmm, bgs_free[tb])
                        ev_si = cS.inc(S.activation(gv, PB[rbg][:, 0:512], AF.Silu))
                        bank_free[rbg] = ev_si
                        b.wait(V, ev_si, ev_umm)
                        ev_h = cV.inc(V.tensor_tensor(
                            bigh[:, hb * TT + tb * 512: hb * TT + (tb + 1) * 512],
                            gv, PB[rbu][:, 0:512], OP.mult))
                        bank_free[rbu] = ev_h
                        bgs_free[tb] = ev_h
                ev_hT = cV.ev()
                if "hT" in tap_t and t == 0:
                    b.wait(Y, ev_hT)
                    b.dch("tp").inc(Y.dma_start(tap_t["hT"][:], bigh[:]), 16)

                ev_chunk = b_chunk
                ev_chunk_free = b_chunk_free
                ev_ds = {}
                for es in range(4):
                    for khc in range(11):
                        cl = es * 11 + khc
                        ci = t * 44 + cl
                        sl = ci % 2
                        b.wait(Y, ev_chunk_free.get(ci - 2))
                        ev_chunk[ci] = b.dch(f"bd{sl}").inc(Y.dma_start(
                            wdr[:, sl * 2048:(sl + 1) * 2048],
                            wdc_d[cl * 128:(cl + 1) * 128, :]), 16)
                        for k4 in range(4):
                            kh = khc * 4 + k4
                            for s in range(NSUB):
                                b.wait(T, ev_chunk[ci], ev_hT,
                                       bank_free[s] if kh == 0 else None)
                                ins = T.matmul(PB[s][:, 0:512],
                                               bigh[:, kh * TT + s * 128: kh * TT + (s + 1) * 128],
                                               wdr[:, sl * 2048 + k4 * 512: sl * 2048 + (k4 + 1) * 512],
                                               start=(kh == 0), stop=(kh == NHB - 1),
                                               skip_group_check=True)
                        ev_chunk_free[ci] = cT.inc(ins)
                    ev_es = cT.ev()
                    for s in range(NSUB):
                        dsl = big2[:, s * 2048 + es * 512: s * 2048 + (es + 1) * 512]
                        b.wait(V, ev_es)
                        ev_rs = cV.inc(V.reduce_sum(bstat[:, 64 + s * 4 + es:65 + s * 4 + es],
                                                    PB[s][:, 0:512], axis=AX))
                        b.wait(S, ev_rs)
                        ev_ev = cS.inc(S.activation(dsl, PB[s][:, 0:512], AF.Copy))
                        bank_free[s] = [ev_ev, ev_rs]
                        ev_ds[(s, es)] = [ev_ev, ev_rs]

                for s in range(NSUB):
                    for es in range(4):
                        b.wait(V, *ev_ds[(s, es)])
                    dsl_full = big2[:, s * 2048:(s + 1) * 2048]
                    cV.inc(V.scalar_tensor_tensor(bigh[:, 0:D], dsl_full, 1.0, dsl_full,
                                                  OP.mult, OP.mult, accum_out=bstat[:, 96:97]))
                    ev_redB = cV.inc(V.tensor_reduce(bstat[:, 97:98], bstat[:, 64 + s * 4:64 + s * 4 + 4],
                                                     AX, OP.add))
                    b.wait(V, ev_redB)
                    ev_muB = cV.inc(V.tensor_scalar(bstat[:, 98:99], bstat[:, 97:98], RD, None, OP.mult))
                    ev_msqB = cV.inc(V.tensor_scalar(bstat[:, 99:100], bstat[:, 96:97], RD, None, OP.mult))
                    b.wait(V, ev_msqB)
                    ev_nv = cV.inc(V.scalar_tensor_tensor(bstat[:, 100:101], bstat[:, 98:99],
                                                          bstat[:, 98:99], bstat[:, 99:100],
                                                          OP.mult, OP.subtract))
                    b.wait(S, ev_nv)
                    ev_sg = cS.inc(S.activation(bstat[:, 101:102], bstat[:, 100:101], AF.Ln,
                                                bias=eps_col[:], scale=neg_col[:]))
                    b.wait(S, ev_sg)
                    evhB = cS.inc(S.mul(bstat[:, 101:102], bstat[:, 101:102], -0.5))
                    b.wait(S, evhB)
                    ev_rcB = cS.inc(S.activation(bstat[:, 102:103], bstat[:, 101:102], AF.Exp))
                    b.wait(S, ev_rcB)
                    ev_t2 = cS.inc(S.mul(bstat[:, 103:104], bstat[:, 98:99], bstat[:, 102:103]))
                    b.wait(S, ev_t2)
                    ev_nb = cS.inc(S.mul(bstat[:, 103:104], bstat[:, 103:104], -1.0))
                    ssl = s % 2
                    slv = bxstg[:, (2 + ssl) * D:(3 + ssl) * D]
                    b.wait(G, b_slot_free[2 + ssl])
                    ev_xr = b.dch(f"bx{2 + ssl}").inc(G.dma_start(
                        slv, x2_d[r0 + s * 128: r0 + (s + 1) * 128, :]), 16)
                    b.wait(V, ev_nb, ev_xr)
                    ev_p = cV.inc(V.tensor_scalar(slv, slv, bstat[:, 103:104], None, OP.add))
                    b.wait(V, ev_p)
                    ev_o = cV.inc(V.scalar_tensor_tensor(slv, dsl_full, bstat[:, 102:103], slv,
                                                         OP.mult, OP.add))
                    b_last_stt[0] = ev_o
                    b.wait(G, ev_o)
                    ev_w = b.dch(f"bx{2 + ssl}").inc(G.dma_start(
                        outf[r0 + s * 128: r0 + (s + 1) * 128, :], slv), 16)
                    b_slot_free[2 + ssl] = ev_w

                nc.leave_named_scope(f"B{t}", _scB[0], False)
                if keep_barriers or t == n_b_tiles - 1:
                    b.quiesce()
                    b.quiesce(G)
                    b.barrier()
                    bank_free = [None] * 8

        b.quiesce()
        b.quiesce(G)

    nc.finalize()
    return nc


def kernel(**inputs):
    x = np.ascontiguousarray(inputs["L0_attn_x"])
    nc = build()
    wk = [k for k in inputs if k != "L0_attn_x"]
    in_maps = []
    for c in range(8):
        m = {k: np.ascontiguousarray(inputs[k]) for k in wk}
        m["L0_attn_x"] = np.ascontiguousarray(x[2 * c:2 * c + 2])
        in_maps.append(m)
    res = run_bass_kernel_spmd(nc, in_maps, core_ids=list(range(8)))
    out = np.concatenate([res.results[c]["out"] for c in range(8)], axis=0)
    return out.astype(np.float32)
